# version 09
# speedup vs baseline: 2.1733x; 1.0010x over previous
import contextlib
import math
import numpy as np
import ml_dtypes
import concourse.bass as bass
import concourse.mybir as mybir
from concourse.bass_utils import run_bass_kernel_spmd
from concourse.alu_op_type import AluOpType

F32 = mybir.dt.float32
BF16 = mybir.dt.bfloat16
I32 = mybir.dt.int32
AF = mybir.ActivationFunctionType
AX = mybir.AxisListType
BFDT = ml_dtypes.bfloat16
OP = AluOpType

B, T, D = 4, 4096, 1024
DK = 128
NEXP, RNK = 8, 64
HID = 3072
LN_EPS = 1e-5
SCALE_QK = 128.0 ** -0.5
INV127 = 1.0 / 127.0
INVD = 1.0 / 1024.0
LNG = math.log(0.99)
NCH = D // 128
NW = T // 512


def g_pow(k):
    return math.exp(LNG * k)



def _box(ap):
    t = ap.tensor
    name = t.name
    shape = list(t.shape)
    kind = type(t).__name__
    ent = [list(x) for x in ap.ap]
    off = int(ap.offset)
    if kind == "DRamTensorHandle":
        hi = off
        for st, cn in ent:
            hi += abs(int(st)) * (int(cn) - 1)
        return (name, 0, 1, off, hi + 1)
    fsz = 1
    for s in shape[1:]:
        fsz *= s
    p0 = off // fsz
    c0 = off % fsz
    pstep, pcnt = int(ent[0][0]), int(ent[0][1])
    p1 = p0 + (pcnt - 1) * (pstep // fsz if pcnt > 1 else 1) + 1
    cspan = 0
    for stp, cn in ent[1:]:
        cspan += abs(int(stp)) * (int(cn) - 1)
    return (name, p0, p1, c0, c0 + cspan + 1)


def _overlap(a, b):
    return (a[0] == b[0] and a[1] < b[2] and b[1] < a[2]
            and a[3] < b[4] and b[3] < a[4])


def _covers(a, b):
    return (a[0] == b[0] and a[1] <= b[1] and a[2] >= b[2] and a[3] <= b[3]
            and a[4] >= b[4])


class TRK:

    ENG = {"pe": "tk", "act": "sc", "dve": "vc", "pool": "gc"}
    NDMA = 8

    def __init__(self, nc, stack):
        self.nc = nc
        self.sems = {}
        self.cnt = {}
        for s in self.ENG.values():
            self.sems[s] = stack.enter_context(nc.semaphore(s))
            self.cnt[s] = 0
        for q in ("sd", "gd"):
            for i in range(self.NDMA):
                s = f"{q}{i}"
                self.sems[s] = stack.enter_context(nc.semaphore(s))
                self.cnt[s] = 0
        self.dma_i = {"spd": 0, "gpd": 0}
        self.known = {e: {} for e in ("pe", "act", "dve", "pool", "spd", "gpd")}
        self.writes = {}
        self.reads = {}
        self.grp = {}

    def _edges(self, box, is_write, need):
        nm = box[0]
        for b, s, c in self.writes.get(nm, ()):
            if _overlap(b, box):
                need[s] = max(need.get(s, 0), c)
        if is_write:
            for b, s, c in self.reads.get(nm, ()):
                if _overlap(b, box):
                    need[s] = max(need.get(s, 0), c)

    def _record(self, box, is_write, sem, cnt):
        nm = box[0]
        if is_write:
            self.writes.setdefault(nm, [])
            self.writes[nm] = [x for x in self.writes[nm] if not _covers(box, x[0])]
            self.writes[nm].append((box, sem, cnt))
            self.reads.setdefault(nm, [])
            self.reads[nm] = [x for x in self.reads[nm] if not _covers(box, x[0])]
        else:
            self.reads.setdefault(nm, []).append((box, sem, cnt))

    def _wait(self, eng_name, eng, need):
        kn = self.known[eng_name]
        for s, v in sorted(need.items()):
            if eng_name == "pe" and s == "tk":
                continue
            if kn.get(s, 0) >= v:
                continue
            eng.wait_ge(self.sems[s], v)
            kn[s] = v

    def op(self, eng_name, eng, reads, writes, fn, inc=None):
        is_dma = eng_name in ("spd", "gpd")
        if is_dma:
            i = self.dma_i[eng_name]
            self.dma_i[eng_name] = i + 1
            mysem = f"{'sd' if eng_name == 'spd' else 'gd'}{i % self.NDMA}"
            inc = 16
        else:
            mysem = self.ENG[eng_name]
            inc = 1 if inc is None else inc
        rb = [_box(a) for a in reads]
        wb = [_box(a) for a in writes]
        need = {}
        for b in rb:
            self._edges(b, False, need)
        for b in wb:
            self._edges(b, True, need)
        if is_dma:
            need[mysem] = max(need.get(mysem, 0), self.cnt[mysem])
        self._wait(eng_name, eng, need)
        inst = fn()
        self.cnt[mysem] += inc
        cnt = self.cnt[mysem]
        inst.then_inc(self.sems[mysem], inc)
        for b in rb:
            self._record(b, False, mysem, cnt)
        for b in wb:
            self._record(b, True, mysem, cnt)
        return cnt

    def dma(self, eng_name, eng, out, in_):
        return self.op(eng_name, eng, [in_], [out],
                       lambda: eng.dma_start(out=out, in_=in_))

    def mm(self, t, out, lhsT, rhs, start, stop):
        rb = [_box(lhsT), _box(rhs)]
        wbox = _box(out)
        need = {}
        for b in rb:
            self._edges(b, False, need)
        self._edges(wbox, True, need)
        self._wait("pe", t, need)
        inst = t.matmul(out, lhsT, rhs, start=start, stop=stop)
        self.cnt["tk"] += 1
        cnt = self.cnt["tk"]
        inst.then_inc(self.sems["tk"], 1)
        for b in rb:
            self._record(b, False, "tk", cnt)
        self._record(wbox, True, "tk", cnt)
        return inst

    def mm_group(self, t, out, pairs):
        n = len(pairs)
        for i, (lhsT, rhs) in enumerate(pairs):
            self.mm(t, out, lhsT, rhs, start=(i == 0), stop=(i == n - 1))
        return self.cnt["tk"]


class _RecInst:
    def __init__(self):
        self.incs = []

    def then_inc(self, sem, k):
        self.incs.append((sem, k))
        return self


class _Rec:

    def __init__(self):
        self.calls = []

    def __getattr__(self, name):
        def call(*a, **kw):
            ri = _RecInst()
            self.calls.append((name, a, kw, ri))
            return ri
        return call

    def replay(self, eng):
        for name, a, kw, ri in self.calls:
            inst = getattr(eng, name)(*a, **kw)
            for sem, k in ri.incs:
                inst.then_inc(sem, k)


class _FakeBlock:

    def __init__(self):
        self.recs = {k: _Rec() for k in ("pe", "act", "dve", "pool", "spd")}

    def tensor(self, f):
        f(self.recs["pe"])

    def scalar(self, f):
        f(self.recs["act"])

    def vector(self, f):
        f(self.recs["dve"])

    def gpsimd(self, f):
        f(self.recs["pool"])

    def sync(self, f):
        f(self.recs["spd"])



def emit_ln(trk, block, bufs, s_ap, width=512, nch=NCH):
    pb = bufs["pb"]
    sq, mrow, m2row, varrow, rstdrow = (bufs["sqbuf"], bufs["mrow"], bufs["m2row"],
                                        bufs["varrow"], bufs["rstdrow"])
    mbc, rbc, tmp0, tmp1 = bufs["mbc"], bufs["rbc"], bufs["tmp0"], bufs["tmp1"]
    onesb, onesrf = bufs["ones_b"], bufs["onesr_f"]
    b_sy, b_sy2, b_m, b_r = bufs["ln_banks"]
    W = width

    @block.scalar
    def _(s):
        for ch in range(nch):
            trk.op("act", s, [s_ap(ch)], [sq[:, W * ch:W * (ch + 1)]],
                   lambda ch=ch: s.activation(sq[:, W * ch:W * (ch + 1)],
                                              s_ap(ch), AF.Square))

    @block.tensor
    def _(t):
        trk.mm_group(t, pb[b_sy][0:1, 0:W],
                     [(onesb[:], s_ap(ch)) for ch in range(nch)])
        trk.mm_group(t, pb[b_sy2][0:1, 0:W],
                     [(onesb[:], sq[:, W * ch:W * (ch + 1)]) for ch in range(nch)])

    @block.vector
    def _(v):
        trk.op("dve", v, [pb[b_sy][0:1, 0:W]], [mrow[:, 0:W]],
               lambda: v.tensor_scalar(mrow[:, 0:W], pb[b_sy][0:1, 0:W], INVD,
                                       None, OP.mult))
        trk.op("dve", v, [mrow[:, 0:W]], [m2row[:, 0:W]],
               lambda: v.tensor_tensor(m2row[:, 0:W], mrow[:, 0:W], mrow[:, 0:W],
                                       OP.mult))
        trk.op("dve", v, [m2row[:, 0:W]], [m2row[:, 0:W]],
               lambda: v.tensor_scalar(m2row[:, 0:W], m2row[:, 0:W], LN_EPS, None,
                                       OP.subtract))
        trk.op("dve", v, [pb[b_sy2][0:1, 0:W], m2row[:, 0:W]], [varrow[:, 0:W]],
               lambda: v.scalar_tensor_tensor(varrow[:, 0:W], pb[b_sy2][0:1, 0:W],
                                              INVD, m2row[:, 0:W], OP.mult,
                                              OP.subtract))

    @block.scalar
    def _(s):
        trk.op("act", s, [varrow[:, 0:W]], [rstdrow[:, 0:W]],
               lambda: s.activation(rstdrow[:, 0:W], varrow[:, 0:W], AF.Ln))
        trk.op("act", s, [rstdrow[:, 0:W]], [rstdrow[:, 0:W]],
               lambda: s.activation(rstdrow[:, 0:W], rstdrow[:, 0:W], AF.Exp,
                                    scale=-0.5))

    @block.tensor
    def _(t):
        trk.mm_group(t, pb[b_m][0:128, 0:W], [(onesrf[:], mrow[:, 0:W])])
        trk.mm_group(t, pb[b_r][0:128, 0:W], [(onesrf[:], rstdrow[:, 0:W])])

    done = []

    @block.vector
    def _(v):
        trk.op("dve", v, [pb[b_m][0:128, 0:W]], [mbc[:, 0:W]],
               lambda: v.tensor_copy(mbc[:, 0:W], pb[b_m][0:128, 0:W]))
        trk.op("dve", v, [pb[b_r][0:128, 0:W]], [rbc[:, 0:W]],
               lambda: v.tensor_copy(rbc[:, 0:W], pb[b_r][0:128, 0:W]))
        for ch in range(nch):
            tmp = (tmp0, tmp1)[ch % 2]
            trk.op("dve", v, [s_ap(ch), mbc[:, 0:W]], [tmp[:, 0:W]],
                   lambda ch=ch, tmp=tmp: v.tensor_tensor(tmp[:, 0:W], s_ap(ch),
                                                          mbc[:, 0:W], OP.subtract))
            done.append(trk.op("dve", v, [tmp[:, 0:W], rbc[:, 0:W]], [s_ap(ch)],
                               lambda ch=ch, tmp=tmp: v.tensor_tensor(
                                   s_ap(ch), tmp[:, 0:W], rbc[:, 0:W], OP.mult)))
    return done


def _loadw(trk, g, dst, src, nch, grp=4):
    if nch == 1:
        return trk.dma("gpd", g, dst[:], src)
    fw = src.tensor.shape[1]
    for c0 in range(0, nch, grp):
        n = min(grp, nch - c0)
        trk.dma("gpd", g,
                dst[:, fw * c0:fw * (c0 + n)].rearrange("p (c f) -> p c f", c=n),
                src[128 * c0:128 * (c0 + n), :].rearrange("(c p) f -> p c f", c=n))


def common_bufs(nc, e):
    bufs = {}
    bufs["sqbuf"] = e(nc.sbuf_tensor("sqbuf", [128, NCH * 512], BF16))
    for nm in ("mrow", "m2row", "varrow", "rstdrow"):
        bufs[nm] = e(nc.sbuf_tensor(nm, [1, 512], F32))
    bufs["mbc"] = e(nc.sbuf_tensor("mbc", [128, 512], F32))
    bufs["rbc"] = e(nc.sbuf_tensor("rbc", [128, 512], F32))
    bufs["tmp0"] = e(nc.sbuf_tensor("tmp0", [128, 512], BF16))
    bufs["tmp1"] = e(nc.sbuf_tensor("tmp1", [128, 512], BF16))
    bufs["ones_b"] = e(nc.sbuf_tensor("ones_b", [128, 1], BF16))
    bufs["onesr_f"] = e(nc.sbuf_tensor("onesr_f", [1, 128], F32))
    return bufs



def build_A():
    nc = bass.Bass("TRN2", target_bir_lowering=False)
    xT = nc.dram_tensor("xT", [D, T], F32, kind="ExternalInput").ap()
    wq = nc.dram_tensor("wq", [D, DK], F32, kind="ExternalInput").ap()
    wk = nc.dram_tensor("wk", [D, DK], F32, kind="ExternalInput").ap()
    wv = nc.dram_tensor("wv", [D, DK], F32, kind="ExternalInput").ap()
    wo = nc.dram_tensor("wo", [DK, D], F32, kind="ExternalInput").ap()
    wrT = nc.dram_tensor("wrT", [D, NEXP], F32, kind="ExternalInput").ap()
    aT = nc.dram_tensor("aT", [D, NEXP * RNK], F32, kind="ExternalInput").ap()
    bstT = nc.dram_tensor("bstT", [NEXP * RNK, D], F32, kind="ExternalInput").ap()
    emat = nc.dram_tensor("emat", [NEXP, 512], F32, kind="ExternalInput").ap()
    onescol = nc.dram_tensor("onescol", [128, 1], F32, kind="ExternalInput").ap()
    onesrow = nc.dram_tensor("onesrow", [1, 128], F32, kind="ExternalInput").ap()
    x2T = nc.dram_tensor("x2T", [D, T], BF16, kind="ExternalOutput").ap()

    with contextlib.ExitStack() as st_:
        e = st_.enter_context
        S = e(nc.sbuf_tensor("S", [128, NCH * T], BF16))
        QT = e(nc.sbuf_tensor("QT", [128, T], BF16))
        KT = e(nc.sbuf_tensor("KT", [128, T], BF16))
        V = e(nc.sbuf_tensor("V", [128, T], BF16))
        wq_s = e(nc.sbuf_tensor("wq_s", [128, NCH * DK], BF16))
        wk_s = e(nc.sbuf_tensor("wk_s", [128, NCH * DK], BF16))
        wv_s = e(nc.sbuf_tensor("wv_s", [128, NCH * DK], BF16))
        wo_s = e(nc.sbuf_tensor("wo_s", [128, D], BF16))
        wrT_s = e(nc.sbuf_tensor("wrT_s", [128, NCH * NEXP], BF16))
        aT_s = e(nc.sbuf_tensor("aT_s", [128, NCH * 512], BF16))
        bstT_s = e(nc.sbuf_tensor("bstT_s", [128, 4 * D], BF16))
        bstT_g = e(nc.sbuf_tensor("bstT_g", [128, 4 * D], BF16))
        emat_s = e(nc.sbuf_tensor("emat_s", [NEXP, 512], F32))
        one_f = e(nc.sbuf_tensor("one_f", [1, 1], F32))
        masks = e(nc.sbuf_tensor("masks", [128, 4 * 512], BF16))
        iotbuf = e(nc.sbuf_tensor("iotbuf", [128, 4 * 512], F32))
        pt0 = e(nc.sbuf_tensor("pt0", [128, 512], BF16))
        pt1 = e(nc.sbuf_tensor("pt1", [128, 512], BF16))
        outn = e(nc.sbuf_tensor("outn", [128, 512], BF16))
        sumrow = e(nc.sbuf_tensor("sumrow", [1, 512], F32))
        recrow = e(nc.sbuf_tensor("recrow", [1, 512], F32))
        recbc = e(nc.sbuf_tensor("recbc", [128, 512], F32))
        midT = e(nc.sbuf_tensor("midT", [128, 4 * 512], BF16))
        l8 = e(nc.sbuf_tensor("l8", [1, 64], F32))
        g8 = e(nc.sbuf_tensor("g8", [NEXP, 1], F32))
        gcol = e(nc.sbuf_tensor("gcol", [128, 4], F32))
        bufs = common_bufs(nc, e)
        pb = [e(nc.psum_tensor(f"pb{i}", [128, 512], F32)) for i in range(8)]
        bufs["pb"] = pb
        bufs["ln_banks"] = (7, 4, 5, 6)
        onesb, onesrf = bufs["ones_b"], bufs["onesr_f"]

        trk = TRK(nc, st_)
        block = e(nc.Block())

        @block.gpsimd
        def _(g):
            for ch in range(NCH):
                trk.dma("gpd", g, S[:, T * ch:T * (ch + 1)],
                        xT[128 * ch:128 * (ch + 1), :])
            _loadw(trk, g, wq_s, wq, NCH)
            _loadw(trk, g, wk_s, wk, NCH)
            _loadw(trk, g, wv_s, wv, NCH)
            _loadw(trk, g, wo_s, wo, 1)
            _loadw(trk, g, wrT_s, wrT, NCH)
            _loadw(trk, g, aT_s, aT, NCH)
            _loadw(trk, g, bstT_s, bstT, 4)
            trk.dma("gpd", g, onesb[:], onescol)
            for o in range(4):
                trk.op("pool", g, [], [iotbuf[:, 512 * o:512 * (o + 1)]],
                       lambda o=o: g.iota(iotbuf[:, 512 * o:512 * (o + 1)],
                                          pattern=[[1, 512]], base=-128 * o,
                                          channel_multiplier=-1,
                                          allow_small_or_imprecise_dtypes=True))

        @block.sync
        def _(s):
            trk.dma("spd", s, emat_s[:], emat)
            trk.dma("spd", s, onesrf[:], onesrow)
            trk.dma("spd", s, one_f[:], onesrow[:, 0:1])

        @block.vector
        def _(v):
            trk.op("dve", v, [iotbuf[:]], [masks[:]],
                   lambda: v.tensor_scalar(masks[:], iotbuf[:], 0.0, None,
                                           OP.is_ge))

        for w in range(NW):
            @block.tensor
            def _(t, w=w):
                for i, wsb in enumerate((wq_s, wk_s)):
                    trk.mm_group(t, pb[i][:],
                                 [(wsb[:, DK * ch:DK * (ch + 1)],
                                   S[:, T * ch + 512 * w:T * ch + 512 * (w + 1)])
                                  for ch in range(NCH)])

            @block.scalar
            def _(s, w=w):
                for i, dst in enumerate((QT, KT)):
                    trk.op("act", s, [pb[i][:]], [dst[:, 512 * w:512 * (w + 1)]],
                           lambda i=i, dst=dst, w=w: s.copy(
                               dst[:, 512 * w:512 * (w + 1)], pb[i][:]))
        for j in range(T // 128):
            @block.tensor
            def _(t, j=j):
                trk.mm_group(t, pb[2 + (j % 2)][0:128, 0:128],
                             [(S[:, T * ch + 128 * j:T * ch + 128 * (j + 1)],
                               wv_s[:, DK * ch:DK * (ch + 1)]) for ch in range(NCH)])

            @block.vector
            def _(v, j=j):
                trk.op("dve", v, [pb[2 + (j % 2)][0:128, 0:128]],
                       [V[:, 128 * j:128 * (j + 1)]],
                       lambda j=j: v.tensor_copy(V[:, 128 * j:128 * (j + 1)],
                                                 pb[2 + (j % 2)][0:128, 0:128]))

        def s_chunk(w):
            return lambda ch: S[:, T * ch + 512 * w:T * ch + 512 * (w + 1)]

        for w in range(NW):
            npj = 4 * w + 4
            for j in range(npj):
                pt = (pt0, pt1)[j % 2]
                bank = pb[j % 2]

                @block.tensor
                def _(t, w=w, j=j, bank=bank):
                    trk.mm_group(t, bank[:],
                                 [(KT[:, 128 * j:128 * (j + 1)],
                                   QT[:, 512 * w:512 * (w + 1)])])

                @block.scalar
                def _(s, pt=pt, bank=bank):
                    trk.op("act", s, [bank[:]], [pt[:]],
                           lambda: s.activation(pt[:], bank[:], AF.Exp,
                                                scale=SCALE_QK))
                if j >= 4 * w:
                    o = j - 4 * w

                    @block.vector
                    def _(v, o=o, pt=pt):
                        trk.op("dve", v, [pt[:], masks[:, 512 * o:512 * (o + 1)]],
                               [pt[:]],
                               lambda: v.tensor_tensor(
                                   pt[:], pt[:], masks[:, 512 * o:512 * (o + 1)],
                                   OP.mult))

                @block.tensor
                def _(t, j=j, pt=pt, npj=npj):
                    trk.mm(t, pb[2][:], V[:, 128 * j:128 * (j + 1)], pt[:],
                           start=(j == 0), stop=(j == npj - 1))
                    trk.mm(t, pb[3][0:1, 0:512], onesb[:], pt[:],
                           start=(j == 0), stop=(j == npj - 1))

            @block.vector
            def _(v):
                trk.op("dve", v, [pb[3][0:1, 0:512]], [sumrow[:]],
                       lambda: v.tensor_copy(sumrow[:], pb[3][0:1, 0:512]))
                trk.op("dve", v, [sumrow[:]], [recrow[:]],
                       lambda: v.reciprocal(recrow[:], sumrow[:]))

            @block.tensor
            def _(t):
                trk.mm_group(t, pb[4][:], [(onesrf[:], recrow[:])])

            @block.vector
            def _(v):
                trk.op("dve", v, [pb[4][:]], [recbc[:]],
                       lambda: v.tensor_copy(recbc[:], pb[4][:]))
                trk.op("dve", v, [pb[2][:], recbc[:]], [outn[:]],
                       lambda: v.tensor_tensor(outn[:], pb[2][:], recbc[:],
                                               OP.mult))

            for m in range(NCH):
                @block.tensor
                def _(t, m=m):
                    trk.mm_group(t, pb[5 + (m % 2)][:],
                                 [(wo_s[:, 128 * m:128 * (m + 1)], outn[:])])

                @block.vector
                def _(v, w=w, m=m):
                    sap = s_chunk(w)(m)
                    trk.op("dve", v, [pb[5 + (m % 2)][:], sap], [sap],
                           lambda sap=sap, m=m: v.tensor_tensor(
                               sap, pb[5 + (m % 2)][:], sap, OP.add))
            emit_ln(trk, block, bufs, s_chunk(w))

        @block.tensor
        def _(t):
            trk.mm_group(t, pb[0][0:1, 0:NEXP],
                         [(S[:, T * ch:T * ch + 1],
                           wrT_s[:, NEXP * ch:NEXP * (ch + 1)])
                          for ch in range(NCH)])

        @block.vector
        def _(v):
            L = l8
            trk.op("dve", v, [pb[0][0:1, 0:NEXP]], [L[:, 0:8]],
                   lambda: v.tensor_copy(L[:, 0:8], pb[0][0:1, 0:NEXP]))
            trk.op("dve", v, [L[:, 0:8]], [L[:, 8:9]],
                   lambda: v.reduce_max(L[:, 8:9], L[:, 0:8], axis=AX.X))
            trk.op("dve", v, [L[:, 8:9]], [L[:, 9:10]],
                   lambda: v.tensor_scalar(L[:, 9:10], L[:, 8:9], -1.0, None,
                                           OP.mult))
            trk.op("dve", v, [L[:, 0:8], L[:, 8:9]], [L[:, 16:24]],
                   lambda: v.tensor_scalar(L[:, 16:24], L[:, 0:8], L[:, 8:9],
                                           None, OP.is_ge))
            trk.op("dve", v, [L[:, 16:24], L[:, 0:8]], [L[:, 24:32]],
                   lambda: v.scalar_tensor_tensor(L[:, 24:32], L[:, 16:24], -1e30,
                                                  L[:, 0:8], OP.mult, OP.add))
            trk.op("dve", v, [L[:, 24:32]], [L[:, 10:11]],
                   lambda: v.reduce_max(L[:, 10:11], L[:, 24:32], axis=AX.X))
            trk.op("dve", v, [L[:, 0:8], L[:, 10:11]], [L[:, 16:24]],
                   lambda: v.tensor_scalar(L[:, 16:24], L[:, 0:8], L[:, 10:11],
                                           None, OP.is_ge))

        @block.scalar
        def _(s):
            trk.op("act", s, [l8[:, 0:8], l8[:, 9:10]], [l8[:, 32:40]],
                   lambda: s.activation(l8[:, 32:40], l8[:, 0:8], AF.Exp,
                                        bias=l8[:, 9:10]))

        @block.vector
        def _(v):
            L = l8
            trk.op("dve", v, [L[:, 32:40], L[:, 16:24]], [L[:, 40:48]],
                   lambda: v.tensor_tensor(L[:, 40:48], L[:, 32:40], L[:, 16:24],
                                           OP.mult))
            trk.op("dve", v, [L[:, 40:48]], [L[:, 11:12]],
                   lambda: v.reduce_sum(L[:, 11:12], L[:, 40:48], axis=AX.X))
            trk.op("dve", v, [L[:, 11:12]], [L[:, 12:13]],
                   lambda: v.reciprocal(L[:, 12:13], L[:, 11:12]))
            trk.op("dve", v, [L[:, 40:48], L[:, 12:13]], [L[:, 48:56]],
                   lambda: v.tensor_scalar(L[:, 48:56], L[:, 40:48], L[:, 12:13],
                                           None, OP.mult))

        @block.tensor
        def _(t):
            trk.mm_group(t, pb[1][0:NEXP, 0:1], [(l8[:, 48:56], one_f[:])])

        @block.vector
        def _(v):
            trk.op("dve", v, [pb[1][0:NEXP, 0:1]], [g8[:]],
                   lambda: v.tensor_copy(g8[:], pb[1][0:NEXP, 0:1]))

        for i in range(4):
            @block.tensor
            def _(t, i=i):
                trk.mm_group(t, pb[2][0:128, 0:1],
                             [(emat_s[:, 128 * i:128 * (i + 1)], g8[:])])

            @block.vector
            def _(v, i=i):
                trk.op("dve", v, [pb[2][0:128, 0:1]], [gcol[:, i:i + 1]],
                       lambda i=i: v.tensor_copy(gcol[:, i:i + 1],
                                                 pb[2][0:128, 0:1]))

        @block.vector
        def _(v):
            for i in range(4):
                trk.op("dve", v, [bstT_s[:, D * i:D * (i + 1)], gcol[:, i:i + 1]],
                       [bstT_g[:, D * i:D * (i + 1)]],
                       lambda i=i: v.tensor_scalar(bstT_g[:, D * i:D * (i + 1)],
                                                   bstT_s[:, D * i:D * (i + 1)],
                                                   gcol[:, i:i + 1], None,
                                                   OP.mult))

        for w in range(NW):
            for m in range(4):
                @block.tensor
                def _(t, w=w, m=m):
                    trk.mm_group(
                        t, pb[m % 2][:],
                        [(aT_s[:, 512 * ch + 128 * m:512 * ch + 128 * (m + 1)],
                          S[:, T * ch + 512 * w:T * ch + 512 * (w + 1)])
                         for ch in range(NCH)])

                @block.scalar
                def _(s, m=m):
                    trk.op("act", s, [pb[m % 2][:]],
                           [midT[:, 512 * m:512 * (m + 1)]],
                           lambda m=m: s.copy(midT[:, 512 * m:512 * (m + 1)],
                                              pb[m % 2][:]))
            for m in range(NCH):
                @block.tensor
                def _(t, m=m):
                    trk.mm_group(
                        t, pb[5 + (m % 2)][:],
                        [(bstT_g[:, D * rc + 128 * m:D * rc + 128 * (m + 1)],
                          midT[:, 512 * rc:512 * (rc + 1)]) for rc in range(4)])

                @block.vector
                def _(v, w=w, m=m):
                    sap = s_chunk(w)(m)
                    trk.op("dve", v, [pb[5 + (m % 2)][:], sap], [sap],
                           lambda sap=sap, m=m: v.scalar_tensor_tensor(
                               sap, sap, 2.0, pb[5 + (m % 2)][:], OP.mult,
                               OP.add))
            emit_ln(trk, block, bufs, s_chunk(w))

        @block.sync
        def _(s):
            for ch in range(NCH):
                trk.dma("spd", s, x2T[128 * ch:128 * (ch + 1), :],
                        S[:, T * ch:T * (ch + 1)])
            for i in range(TRK.NDMA):
                s.wait_ge(trk.sems[f"sd{i}"], trk.cnt[f"sd{i}"])
    return nc



def build_B():
    nc = bass.Bass("TRN2", target_bir_lowering=False)
    x2Td = nc.dram_tensor("x2T", [D, T], BF16, kind="ExternalInput").ap()
    resG = nc.dram_tensor("resG", [D, 2048], BF16, kind="ExternalInput").ap()
    wq1 = nc.dram_tensor("wq1", [D, D], F32, kind="ExternalInput").ap()
    wk1 = nc.dram_tensor("wk1", [D, D], F32, kind="ExternalInput").ap()
    wv1h = nc.dram_tensor("wv1h", [D, 512], F32, kind="ExternalInput").ap()
    onescol = nc.dram_tensor("onescol", [128, 1], F32, kind="ExternalInput").ap()
    onesrow = nc.dram_tensor("onesrow", [1, 128], F32, kind="ExternalInput").ap()
    x3gT = nc.dram_tensor("x3gT", [D, 2048], BF16, kind="ExternalOutput").ap()

    NWIN = 4
    with contextlib.ExitStack() as st_:
        e = st_.enter_context
        KT = e(nc.sbuf_tensor("KT", [128, NCH * T], BF16))
        V = e(nc.sbuf_tensor("V", [128, 32 * 512], BF16))
        QTw = e(nc.sbuf_tensor("QTw", [128, NCH * 512], BF16))
        Span0 = e(nc.sbuf_tensor("Span0", [128, NCH * 512], BF16))
        wq1_s = e(nc.sbuf_tensor("wq1_s", [128, NCH * D], BF16))
        wk1_s = e(nc.sbuf_tensor("wk1_s", [128, NCH * D], BF16))
        wv1_s = e(nc.sbuf_tensor("wv1_s", [128, NCH * 512], BF16))
        adt0 = e(nc.sbuf_tensor("adt0", [128, 512], BF16))
        adt1 = e(nc.sbuf_tensor("adt1", [128, 512], BF16))
        gtq = e(nc.sbuf_tensor("gtq", [128, 512], BF16))
        gts = e(nc.sbuf_tensor("gts", [128, 512], BF16))
        dcorr = e(nc.sbuf_tensor("dcorr", [128, 4 * 512], F32))
        iotb = e(nc.sbuf_tensor("iotb", [128, 2 * 512], F32))
        resid = e(nc.sbuf_tensor("resid", [128, NCH * 512], BF16))
        y3 = e(nc.sbuf_tensor("y3", [128, NCH * 512], BF16))
        bufs = common_bufs(nc, e)
        pb = [e(nc.psum_tensor(f"pb{i}", [128, 512], F32)) for i in range(8)]
        bufs["pb"] = pb
        bufs["ln_banks"] = (7, 4, 5, 6)
        onesb, onesrf = bufs["ones_b"], bufs["onesr_f"]

        trk = TRK(nc, st_)
        block = e(nc.Block())

        @block.gpsimd
        def _(g):
            trk.dma("gpd", g, onesb[:], onescol)
            trk.op("pool", g, [], [iotb[:, 0:512]],
                   lambda: g.iota(iotb[:, 0:512], pattern=[[1, 512]], base=0,
                                  channel_multiplier=0,
                                  allow_small_or_imprecise_dtypes=True))
            trk.op("pool", g, [], [iotb[:, 512:1024]],
                   lambda: g.iota(iotb[:, 512:1024], pattern=[[0, 4], [1, 128]],
                                  base=0, channel_multiplier=0,
                                  allow_small_or_imprecise_dtypes=True))
            for o in range(4):
                trk.op("pool", g, [], [dcorr[:, 512 * o:512 * (o + 1)]],
                       lambda o=o: g.iota(
                           dcorr[:, 512 * o:512 * (o + 1)],
                           pattern=[[1, 512]], base=-128 * o,
                           channel_multiplier=-1,
                           allow_small_or_imprecise_dtypes=True))
            _loadw(trk, g, wq1_s, wq1, NCH)
            _loadw(trk, g, wk1_s, wk1, NCH)
            _loadw(trk, g, wv1_s, wv1h, NCH)

        @block.sync
        def _(s):
            trk.dma("spd", s, onesrf[:], onesrow)

        @block.scalar
        def _(s):
            trk.op("act", s, [iotb[:, 0:512]], [gtq[:]],
                   lambda: s.activation(gtq[:], iotb[:, 0:512], AF.Exp, scale=LNG))
            trk.op("act", s, [iotb[:, 512:1024]], [gts[:]],
                   lambda: s.activation(gts[:], iotb[:, 512:1024], AF.Exp,
                                        scale=-LNG))

        @block.vector
        def _(v):
            for o in range(4):
                dst = dcorr[:, 512 * o:512 * (o + 1)]
                trk.op("dve", v, [dst], [dst],
                       lambda dst=dst: v.tensor_scalar(
                           dst, dst, 0.0, None, OP.is_ge))
                trk.op("dve", v, [dst], [dst],
                       lambda dst=dst, o=o: v.tensor_scalar(
                           dst, dst, g_pow(-128 * o), None, OP.mult))

        for pan in range(NW):
            Sp = Span0

            @block.sync
            def _(s, pan=pan, Sp=Sp):
                for ch in range(NCH):
                    trk.dma("spd", s, Sp[:, 512 * ch:512 * (ch + 1)],
                            x2Td[128 * ch:128 * (ch + 1),
                                 512 * pan:512 * (pan + 1)])

            for m in range(NCH):
                @block.tensor
                def _(t, Sp=Sp, m=m):
                    trk.mm_group(t, pb[0 + (m % 2)][:],
                                 [(wk1_s[:, D * ch + 128 * m:D * ch + 128 * (m + 1)],
                                   Sp[:, 512 * ch:512 * (ch + 1)])
                                  for ch in range(NCH)])

                @block.vector
                def _(v, pan=pan, m=m):
                    dk = KT[:, T * m + 512 * pan:T * m + 512 * (pan + 1)]
                    trk.op("dve", v, [pb[0 + (m % 2)][:], gts[:]], [dk],
                           lambda dk=dk, m=m: v.tensor_tensor(
                               dk, pb[0 + (m % 2)][:], gts[:], OP.mult))

            for jj in range(4):
                j = 4 * pan + jj

                @block.tensor
                def _(t, Sp=Sp, jj=jj):
                    trk.mm_group(
                        t, pb[2 + (jj % 2)][:],
                        [(Sp[:, 512 * ch + 128 * jj:512 * ch + 128 * (jj + 1)],
                          wv1_s[:, 512 * ch:512 * (ch + 1)]) for ch in range(NCH)])

                @block.scalar
                def _(s, jj=jj, j=j):
                    trk.op("act", s, [pb[2 + (jj % 2)][:]],
                           [V[:, 512 * j:512 * (j + 1)]],
                           lambda jj=jj, j=j: s.copy(V[:, 512 * j:512 * (j + 1)],
                                                     pb[2 + (jj % 2)][:]))

        for w in range(NWIN):
            @block.sync
            def _(s, w=w):
                for ch in range(NCH):
                    trk.dma("spd", s, resid[:, 512 * ch:512 * (ch + 1)],
                            resG[128 * ch:128 * (ch + 1), 512 * w:512 * (w + 1)])

            for hm in range(2):
                pan_t = 2 * w + hm
                @block.sync
                def _(s, pan_t=pan_t):
                    for ch in range(NCH):
                        trk.dma("spd", s, Span0[:, 512 * ch:512 * (ch + 1)],
                                x2Td[128 * ch:128 * (ch + 1),
                                     512 * pan_t:512 * (pan_t + 1)])

                for m in range(NCH):
                    @block.tensor
                    def _(t, m=m):
                        trk.mm_group(
                            t, pb[0 + (m % 2)][:],
                            [(wq1_s[:, D * ch + 128 * m:D * ch + 128 * (m + 1)],
                              Span0[:, 512 * ch:512 * (ch + 1)])
                             for ch in range(NCH)])

                    @block.vector
                    def _(v, m=m):
                        dq = QTw[:, 512 * m:512 * (m + 1)]
                        trk.op("dve", v, [pb[0 + (m % 2)][:], gtq[:]], [dq],
                               lambda dq=dq, m=m: v.tensor_tensor(
                                   dq, pb[0 + (m % 2)][:], gtq[:], OP.mult))
                for j in range(4 * pan_t + 4):
                    abuf = (adt0, adt1)[j % 2]
                    ab = 4 + (j % 2)

                    @block.tensor
                    def _(t, j=j, ab=ab):
                        trk.mm_group(
                            t, pb[ab][:],
                            [(KT[:, T * ch + 128 * j:T * ch + 128 * (j + 1)],
                              QTw[:, 512 * ch:512 * (ch + 1)])
                             for ch in range(NCH)])

                    if j < 4 * pan_t:
                        sc_t = g_pow(512 * pan_t - 128 * j)

                        @block.scalar
                        def _(s, ab=ab, abuf=abuf, sc_t=sc_t):
                            trk.op("act", s, [pb[ab][:]], [abuf[:]],
                                   lambda: s.activation(abuf[:], pb[ab][:],
                                                        AF.Copy, scale=sc_t))
                    else:
                        o = j - 4 * pan_t

                        @block.vector
                        def _(v, ab=ab, abuf=abuf, o=o):
                            dm = dcorr[:, 512 * o:512 * (o + 1)]
                            trk.op("dve", v, [pb[ab][:], dm], [abuf[:]],
                                   lambda: v.tensor_tensor(abuf[:], pb[ab][:],
                                                           dm, OP.mult))

                    @block.tensor
                    def _(t, j=j, abuf=abuf, pan_t=pan_t):
                        last = (j == 4 * pan_t + 3)
                        for mc in range(4):
                            trk.mm(t, pb[mc][:],
                                   abuf[:, 128 * mc:128 * (mc + 1)],
                                   V[:, 512 * j:512 * (j + 1)],
                                   start=(j == 0), stop=last)

                for mc in range(4):
                    @block.vector
                    def _(v, hm=hm, mc=mc):
                        ch = 4 * hm + mc
                        dst = y3[:, 512 * ch:512 * (ch + 1)]
                        src = resid[:, 512 * ch:512 * (ch + 1)]
                        trk.op("dve", v, [pb[mc][:], src], [dst],
                               lambda dst=dst, src=src, mc=mc: v.tensor_tensor(
                                   dst, pb[mc][:], src, OP.add))

            emit_ln(trk, block, bufs,
                    lambda ch: y3[:, 512 * ch:512 * (ch + 1)])

            @block.sync
            def _(s, w=w):
                for ch in range(NCH):
                    trk.dma("spd", s, x3gT[128 * ch:128 * (ch + 1),
                                           512 * w:512 * (w + 1)],
                            y3[:, 512 * ch:512 * (ch + 1)])

        @block.sync
        def _(s):
            for i in range(TRK.NDMA):
                s.wait_ge(trk.sems[f"sd{i}"], trk.cnt[f"sd{i}"])
    return nc



def build_C():
    nc = bass.Bass("TRN2", target_bir_lowering=False)
    x3 = nc.dram_tensor("x3gT", [D, 2048], BF16, kind="ExternalInput").ap()
    wgT = nc.dram_tensor("wgT", [D, HID], I32, kind="ExternalInput").ap()
    wuT = nc.dram_tensor("wuT", [D, HID], I32, kind="ExternalInput").ap()
    wdT = nc.dram_tensor("wdT", [HID, D], I32, kind="ExternalInput").ap()
    onescol = nc.dram_tensor("onescol", [128, 1], F32, kind="ExternalInput").ap()
    onesrow = nc.dram_tensor("onesrow", [1, 128], F32, kind="ExternalInput").ap()
    x4T = nc.dram_tensor("x4T", [D, 2048], F32, kind="ExternalOutput").ap()

    NH = HID // 128
    PW = 256
    NPAN = 2048 // PW
    with contextlib.ExitStack() as st_:
        e = st_.enter_context
        wg_s = e(nc.sbuf_tensor("wg_s", [128, NCH * HID], BF16))
        wu_s = e(nc.sbuf_tensor("wu_s", [128, NCH * HID], BF16))
        wd_s = e(nc.sbuf_tensor("wd_s", [128, NH * D], BF16))
        sp0 = e(nc.sbuf_tensor("sp0", [128, NCH * PW], BF16))
        sp1 = e(nc.sbuf_tensor("sp1", [128, NCH * PW], BF16))
        gs = e(nc.sbuf_tensor("gs", [128, PW], BF16))
        us = e(nc.sbuf_tensor("us", [128, PW], BF16))
        hs = e(nc.sbuf_tensor("hs", [128, NH * PW], BF16))
        y4 = e(nc.sbuf_tensor("y4", [128, NCH * PW], F32))
        y4b = e(nc.sbuf_tensor("y4b", [128, NCH * PW], BF16))
        outf = e(nc.sbuf_tensor("outf", [128, NCH * PW], F32))
        bufs = common_bufs(nc, e)
        pb = [e(nc.psum_tensor(f"pb{i}", [128, 512], F32)) for i in range(8)]
        bufs["pb"] = pb
        bufs["ln_banks"] = (6, 7, 4, 5)
        onesb, onesrf = bufs["ones_b"], bufs["onesr_f"]

        trk = TRK(nc, st_)
        block = e(nc.Block())

        @block.gpsimd
        def _(g):
            trk.dma("gpd", g, onesb[:], onescol)
            _loadw(trk, g, wg_s, wgT, NCH, grp=2)
            _loadw(trk, g, wu_s, wuT, NCH, grp=2)
            _loadw(trk, g, wd_s, wdT, NH, grp=4)

        @block.sync
        def _(s):
            trk.dma("spd", s, onesrf[:], onesrow)

        for pan in range(NPAN):
            Sp = (sp0, sp1)[pan % 2]

            @block.sync
            def _(s, pan=pan, Sp=Sp):
                for ch in range(NCH):
                    trk.dma("spd", s, Sp[:, PW * ch:PW * (ch + 1)],
                            x3[128 * ch:128 * (ch + 1), PW * pan:PW * (pan + 1)])

            def scol(ch, Sp=Sp):
                return Sp[:, PW * ch:PW * (ch + 1)]

            for hh in range(NH):
                bg = 0 + (hh % 2)
                bu = 2 + (hh % 2)

                @block.tensor
                def _(t, hh=hh, bg=bg, bu=bu, Sp=Sp):
                    trk.mm_group(t, pb[bg][0:128, 0:PW],
                                 [(wg_s[:, HID * ch + 128 * hh:HID * ch + 128 * (hh + 1)],
                                   scol(ch, Sp)) for ch in range(NCH)])
                    trk.mm_group(t, pb[bu][0:128, 0:PW],
                                 [(wu_s[:, HID * ch + 128 * hh:HID * ch + 128 * (hh + 1)],
                                   scol(ch, Sp)) for ch in range(NCH)])

                @block.scalar
                def _(s, bg=bg):
                    trk.op("act", s, [pb[bg][0:128, 0:PW]], [gs[:]],
                           lambda: s.activation(gs[:], pb[bg][0:128, 0:PW],
                                                AF.Silu, scale=INV127))

                @block.vector
                def _(v, bu=bu, hh=hh):
                    trk.op("dve", v, [pb[bu][0:128, 0:PW]], [us[:]],
                           lambda: v.tensor_scalar(us[:], pb[bu][0:128, 0:PW],
                                                   INV127, None, OP.mult))
                    trk.op("dve", v, [gs[:], us[:]], [hs[:, PW * hh:PW * (hh + 1)]],
                           lambda hh=hh: v.tensor_tensor(
                               hs[:, PW * hh:PW * (hh + 1)], gs[:], us[:],
                               OP.mult))

            for mc in range(NCH):
                bd = 0 + (mc % 2)

                @block.tensor
                def _(t, mc=mc, bd=bd):
                    trk.mm_group(t, pb[bd][0:128, 0:PW],
                                 [(wd_s[:, D * hh + 128 * mc:D * hh + 128 * (mc + 1)],
                                   hs[:, PW * hh:PW * (hh + 1)])
                                  for hh in range(NH)])

                @block.vector
                def _(v, mc=mc, bd=bd, Sp=Sp):
                    dst = y4[:, PW * mc:PW * (mc + 1)]
                    trk.op("dve", v, [pb[bd][0:128, 0:PW], scol(mc, Sp)], [dst],
                           lambda dst=dst, mc=mc, bd=bd, Sp=Sp:
                           v.scalar_tensor_tensor(dst, pb[bd][0:128, 0:PW],
                                                  INV127, scol(mc, Sp),
                                                  OP.mult, OP.add))
                    trk.op("dve", v, [dst], [y4b[:, PW * mc:PW * (mc + 1)]],
                           lambda dst=dst, mc=mc: v.tensor_copy(
                               y4b[:, PW * mc:PW * (mc + 1)], dst))

            mrow, m2row, varrow, rstdrow = (bufs["mrow"], bufs["m2row"],
                                            bufs["varrow"], bufs["rstdrow"])
            mbc, rbc = bufs["mbc"], bufs["rbc"]
            sq = bufs["sqbuf"]

            @block.tensor
            def _(t):
                trk.mm_group(t, pb[6][0:1, 0:PW],
                             [(onesb[:], y4b[:, PW * ch:PW * (ch + 1)])
                              for ch in range(NCH)])

            for ch in range(NCH):
                sq2 = sq[:, 512 * (ch % 2):512 * (ch % 2) + PW]

                @block.scalar
                def _(s, ch=ch, sq2=sq2):
                    trk.op("act", s, [y4b[:, PW * ch:PW * (ch + 1)]], [sq2],
                           lambda: s.activation(
                               sq2, y4b[:, PW * ch:PW * (ch + 1)], AF.Square))

                @block.tensor
                def _(t, ch=ch, sq2=sq2):
                    trk.mm(t, pb[7][0:1, 0:PW], onesb[:], sq2,
                           start=(ch == 0), stop=(ch == NCH - 1))

            @block.vector
            def _(v):
                trk.op("dve", v, [pb[6][0:1, 0:PW]], [mrow[:, 0:PW]],
                       lambda: v.tensor_scalar(mrow[:, 0:PW], pb[6][0:1, 0:PW],
                                               INVD, None, OP.mult))
                trk.op("dve", v, [mrow[:, 0:PW]], [m2row[:, 0:PW]],
                       lambda: v.tensor_tensor(m2row[:, 0:PW], mrow[:, 0:PW],
                                               mrow[:, 0:PW], OP.mult))
                trk.op("dve", v, [m2row[:, 0:PW]], [m2row[:, 0:PW]],
                       lambda: v.tensor_scalar(m2row[:, 0:PW], m2row[:, 0:PW],
                                               LN_EPS, None, OP.subtract))
                trk.op("dve", v, [pb[7][0:1, 0:PW], m2row[:, 0:PW]],
                       [varrow[:, 0:PW]],
                       lambda: v.scalar_tensor_tensor(
                           varrow[:, 0:PW], pb[7][0:1, 0:PW], INVD,
                           m2row[:, 0:PW], OP.mult, OP.subtract))

            @block.scalar
            def _(s):
                trk.op("act", s, [varrow[:, 0:PW]], [rstdrow[:, 0:PW]],
                       lambda: s.activation(rstdrow[:, 0:PW], varrow[:, 0:PW],
                                            AF.Ln))
                trk.op("act", s, [rstdrow[:, 0:PW]], [rstdrow[:, 0:PW]],
                       lambda: s.activation(rstdrow[:, 0:PW], rstdrow[:, 0:PW],
                                            AF.Exp, scale=-0.5))

            @block.tensor
            def _(t):
                trk.mm_group(t, pb[4][0:128, 0:PW], [(onesrf[:], mrow[:, 0:PW])])
                trk.mm_group(t, pb[5][0:128, 0:PW],
                             [(onesrf[:], rstdrow[:, 0:PW])])

            @block.vector
            def _(v):
                trk.op("dve", v, [pb[4][0:128, 0:PW]], [mbc[:, 0:PW]],
                       lambda: v.tensor_copy(mbc[:, 0:PW], pb[4][0:128, 0:PW]))
                trk.op("dve", v, [pb[5][0:128, 0:PW]], [rbc[:, 0:PW]],
                       lambda: v.tensor_copy(rbc[:, 0:PW], pb[5][0:128, 0:PW]))
                for ch in range(NCH):
                    dst = outf[:, PW * ch:PW * (ch + 1)]
                    trk.op("dve", v, [y4[:, PW * ch:PW * (ch + 1)], mbc[:, 0:PW]],
                           [dst],
                           lambda ch=ch, dst=dst: v.tensor_tensor(
                               dst, y4[:, PW * ch:PW * (ch + 1)], mbc[:, 0:PW],
                               OP.subtract))
                    trk.op("dve", v, [dst, rbc[:, 0:PW]], [dst],
                           lambda dst=dst: v.tensor_tensor(
                               dst, dst, rbc[:, 0:PW], OP.mult))

            @block.sync
            def _(s, pan=pan):
                for ch in range(NCH):
                    trk.dma("spd", s, x4T[128 * ch:128 * (ch + 1),
                                          PW * pan:PW * (pan + 1)],
                            outf[:, PW * ch:PW * (ch + 1)])

        @block.sync
        def _(s):
            for i in range(TRK.NDMA):
                s.wait_ge(trk.sems[f"sd{i}"], trk.cnt[f"sd{i}"])
    return nc




def build_F():
    nc = bass.Bass("TRN2", target_bir_lowering=False)
    xT = nc.dram_tensor("xT", [D, T], F32, kind="ExternalInput").ap()
    wq = nc.dram_tensor("wq", [D, DK], F32, kind="ExternalInput").ap()
    wk = nc.dram_tensor("wk", [D, DK], F32, kind="ExternalInput").ap()
    wv = nc.dram_tensor("wv", [D, DK], F32, kind="ExternalInput").ap()
    wo = nc.dram_tensor("wo", [DK, D], F32, kind="ExternalInput").ap()
    wrT = nc.dram_tensor("wrT", [D, NEXP], F32, kind="ExternalInput").ap()
    aT = nc.dram_tensor("aT", [D, NEXP * RNK], F32, kind="ExternalInput").ap()
    bstT = nc.dram_tensor("bstT", [NEXP * RNK, D], F32, kind="ExternalInput").ap()
    emat = nc.dram_tensor("emat", [NEXP, 512], F32, kind="ExternalInput").ap()
    onescol = nc.dram_tensor("onescol", [128, 1], F32, kind="ExternalInput").ap()
    onesrow = nc.dram_tensor("onesrow", [1, 128], F32, kind="ExternalInput").ap()
    hfin = nc.dram_tensor("hfin", [128, 1], F32, kind="ExternalInput").ap()
    wq1 = nc.dram_tensor("wq1", [D, D], F32, kind="ExternalInput").ap()
    wk1 = nc.dram_tensor("wk1", [D, D], F32, kind="ExternalInput").ap()
    wv1h = nc.dram_tensor("wv1h", [D, 512], F32, kind="ExternalInput").ap()
    wgT = nc.dram_tensor("wgT", [D, HID], I32, kind="ExternalInput").ap()
    wuT = nc.dram_tensor("wuT", [D, HID], I32, kind="ExternalInput").ap()
    wdT = nc.dram_tensor("wdT", [HID, D], I32, kind="ExternalInput").ap()
    x2d = nc.dram_tensor("x2d", [D, T], BF16, kind="Internal").ap()
    resGd = nc.dram_tensor("resGd", [D, 2048], BF16, kind="Internal").ap()
    x3d = nc.dram_tensor("x3d", [D, 2048], BF16, kind="Internal").ap()
    x4T = nc.dram_tensor("x4T", [D, 2048], F32, kind="ExternalOutput").ap()

    NWIN = 4
    NH = HID // 128
    PW = 256
    NPAN = 2048 // PW

    with contextlib.ExitStack() as st_:
        e = st_.enter_context
        ARB_ = e(nc.sbuf_tensor("ARB", [128, 88576], BF16))
        ARF_ = e(nc.sbuf_tensor("ARF", [128, 3584], F32))
        ARB = ARB_[:]
        ARF = ARF_[:]
        S = ARB[:, 0:32768]
        QT = ARB[:, 32768:36864]
        KT0 = ARB[:, 36864:40960]
        V0 = ARB[:, 40960:45056]
        wq_s = ARB[:, 45056:46080]
        wk_s = ARB[:, 46080:47104]
        wv_s = ARB[:, 47104:48128]
        wo_s = ARB[:, 48128:49152]
        wrT_s = ARB[:, 49152:49216]
        aT_s = ARB[:, 49216:53312]
        bstT_s = ARB[:, 53312:57408]
        bstT_g = ARB[:, 57408:61504]
        midT = ARB[:, 61504:63552]
        masks = ARB[:, 63552:65600]
        pt0 = ARB[:, 65600:66112]
        pt1 = ARB[:, 66112:66624]
        pt2 = ARB[:, 67136:67648]
        outn = ARB[:, 66624:67136]
        KT1 = ARB[:, 0:32768]
        V1 = ARB[:, 32768:49152]
        wk1_s = ARB[:, 49152:57344]
        wv1_s = ARB[:, 57344:61440]
        wq1_s = ARB[:, 61440:69632]
        QTw = ARB[:, 69632:73728]
        Span = ARB[:, 73728:77824]
        adt0 = ARB[:, 77824:78336]
        adt1 = ARB[:, 78336:78848]
        gtq = ARB[:, 78848:79360]
        gts = ARB[:, 79360:79872]
        resid = ARB[:, 80384:84480]
        y3 = ARB[:, 84480:88576]
        wg_s = ARB[:, 0:24576]
        wu_s = ARB[:, 24576:49152]
        wd_s = ARB[:, 49152:73728]
        hs = ARB[:, 73728:79872]
        sp0 = ARB[:, 79872:81920]
        sp1 = ARB[:, 81920:83968]
        y4b = ARB[:, 83968:86016]
        gs = ARB[:, 86016:86272]
        us = ARB[:, 86272:86528]
        es_ = ARB[:, 86528:86784]
        es2_ = ARB[:, 86784:87040]
        gu_ = ARB[:, 87040:87296]
        mbc = ARF[:, 0:512]
        rbc = ARF[:, 512:1024]
        recbc = ARF[:, 1024:1536]
        den_ = ARF[:, 1024:1280]
        rc_ = ARF[:, 1280:1536]
        iof = ARF[:, 1536:3584]
        dcorr = ARF[:, 1536:3584]
        y4 = ARF[:, 1536:3584]
        sqbuf = e(nc.sbuf_tensor("sqbuf", [128, 2 * 512], BF16))
        tmp0 = e(nc.sbuf_tensor("tmp0", [128, 512], BF16))
        tmp1 = e(nc.sbuf_tensor("tmp1", [128, 512], BF16))
        mrow = e(nc.sbuf_tensor("mrow", [1, 512], F32))
        m2row = e(nc.sbuf_tensor("m2row", [1, 512], F32))
        varrow = e(nc.sbuf_tensor("varrow", [1, 512], F32))
        rstdrow = e(nc.sbuf_tensor("rstdrow", [1, 512], F32))
        ones_b = e(nc.sbuf_tensor("ones_b", [128, 1], BF16))
        onesr_f = e(nc.sbuf_tensor("onesr_f", [1, 128], F32))
        one_f = e(nc.sbuf_tensor("one_f", [1, 1], F32))
        l8 = e(nc.sbuf_tensor("l8", [1, 64], F32))
        g8 = e(nc.sbuf_tensor("g8", [NEXP, 1], F32))
        gcol = e(nc.sbuf_tensor("gcol", [128, 4], F32))
        emat_s = e(nc.sbuf_tensor("emat_s", [NEXP, 512], F32))
        hf = e(nc.sbuf_tensor("hf", [128, 1], F32))
        pb = [e(nc.psum_tensor(f"pb{i}", [128, 512], F32)) for i in range(8)]

        bufs = dict(pb=pb, sqbuf=sqbuf, mrow=mrow, m2row=m2row, varrow=varrow,
                    rstdrow=rstdrow, mbc=mbc, rbc=rbc, tmp0=tmp0, tmp1=tmp1,
                    ones_b=ones_b, onesr_f=onesr_f, ln_banks=(7, 4, 5, 6))
        trk = TRK(nc, st_)
        block = _FakeBlock()

        def emit_ln_f(s_ap, width=512):
            W = width
            b_sy, b_sy2, b_m, b_r = bufs["ln_banks"]

            @block.tensor
            def _(t):
                trk.mm_group(t, pb[b_sy][0:1, 0:W],
                             [(ones_b[:], s_ap(ch)) for ch in range(NCH)])

            for ch in range(NCH):
                sq = sqbuf[:, 512 * (ch % 2):512 * (ch % 2) + W]

                @block.scalar
                def _(s, ch=ch, sq=sq):
                    trk.op("act", s, [s_ap(ch)], [sq],
                           lambda: s.activation(sq, s_ap(ch), AF.Square))

                @block.tensor
                def _(t, ch=ch, sq=sq):
                    trk.mm(t, pb[b_sy2][0:1, 0:W], ones_b[:], sq,
                           start=(ch == 0), stop=(ch == NCH - 1))

            @block.vector
            def _(v):
                trk.op("dve", v, [pb[b_sy][0:1, 0:W]], [mrow[:, 0:W]],
                       lambda: v.tensor_scalar(mrow[:, 0:W], pb[b_sy][0:1, 0:W],
                                               INVD, None, OP.mult))
                trk.op("dve", v, [mrow[:, 0:W]], [m2row[:, 0:W]],
                       lambda: v.tensor_tensor(m2row[:, 0:W], mrow[:, 0:W],
                                               mrow[:, 0:W], OP.mult))
                trk.op("dve", v, [m2row[:, 0:W]], [m2row[:, 0:W]],
                       lambda: v.tensor_scalar(m2row[:, 0:W], m2row[:, 0:W],
                                               LN_EPS, None, OP.subtract))
                trk.op("dve", v, [pb[b_sy2][0:1, 0:W], m2row[:, 0:W]],
                       [varrow[:, 0:W]],
                       lambda: v.scalar_tensor_tensor(
                           varrow[:, 0:W], pb[b_sy2][0:1, 0:W], INVD,
                           m2row[:, 0:W], OP.mult, OP.subtract))

            @block.scalar
            def _(s):
                trk.op("act", s, [varrow[:, 0:W]], [rstdrow[:, 0:W]],
                       lambda: s.activation(rstdrow[:, 0:W], varrow[:, 0:W],
                                            AF.Ln))
                trk.op("act", s, [rstdrow[:, 0:W]], [rstdrow[:, 0:W]],
                       lambda: s.activation(rstdrow[:, 0:W], rstdrow[:, 0:W],
                                            AF.Exp, scale=-0.5))

            @block.tensor
            def _(t):
                trk.mm_group(t, pb[b_m][0:128, 0:W], [(onesr_f[:], mrow[:, 0:W])])
                trk.mm_group(t, pb[b_r][0:128, 0:W],
                             [(onesr_f[:], rstdrow[:, 0:W])])

            @block.vector
            def _(v):
                trk.op("dve", v, [pb[b_m][0:128, 0:W]], [mbc[:, 0:W]],
                       lambda: v.tensor_copy(mbc[:, 0:W], pb[b_m][0:128, 0:W]))
                trk.op("dve", v, [pb[b_r][0:128, 0:W]], [rbc[:, 0:W]],
                       lambda: v.tensor_copy(rbc[:, 0:W], pb[b_r][0:128, 0:W]))
                for ch in range(NCH):
                    tmp = (tmp0, tmp1)[ch % 2]
                    trk.op("dve", v, [s_ap(ch), mbc[:, 0:W]], [tmp[:, 0:W]],
                           lambda ch=ch, tmp=tmp: v.tensor_tensor(
                               tmp[:, 0:W], s_ap(ch), mbc[:, 0:W], OP.subtract))
                    trk.op("dve", v, [tmp[:, 0:W], rbc[:, 0:W]], [s_ap(ch)],
                           lambda ch=ch, tmp=tmp: v.tensor_tensor(
                               s_ap(ch), tmp[:, 0:W], rbc[:, 0:W], OP.mult))

        @block.gpsimd
        def _(g):
            for ch in range(NCH):
                trk.dma("gpd", g, S[:, T * ch:T * (ch + 1)],
                        xT[128 * ch:128 * (ch + 1), :])
            _loadw(trk, g, wq_s, wq, NCH)
            _loadw(trk, g, wk_s, wk, NCH)
            _loadw(trk, g, wv_s, wv, NCH)
            _loadw(trk, g, wo_s, wo, 1)
            _loadw(trk, g, wrT_s, wrT, NCH)
            _loadw(trk, g, aT_s, aT, NCH)
            _loadw(trk, g, bstT_s, bstT, 4)
            trk.dma("gpd", g, ones_b[:], onescol)
            trk.dma("gpd", g, hf[:], hfin)
            for o in range(4):
                trk.op("pool", g, [], [iof[:, 512 * o:512 * (o + 1)]],
                       lambda o=o: g.iota(iof[:, 512 * o:512 * (o + 1)],
                                          pattern=[[1, 512]], base=-128 * o,
                                          channel_multiplier=-1,
                                          allow_small_or_imprecise_dtypes=True))

        @block.sync
        def _(s):
            trk.dma("spd", s, emat_s[:], emat)
            trk.dma("spd", s, onesr_f[:], onesrow)
            trk.dma("spd", s, one_f[:], onesrow[:, 0:1])

        @block.vector
        def _(v):
            trk.op("dve", v, [iof[:]], [masks[:]],
                   lambda: v.tensor_scalar(masks[:], iof[:], 0.0, None,
                                           OP.is_ge))

        for w in range(NW):
            @block.tensor
            def _(t, w=w):
                for i, wsb in enumerate((wq_s, wk_s)):
                    trk.mm_group(t, pb[i][:],
                                 [(wsb[:, DK * ch:DK * (ch + 1)],
                                   S[:, T * ch + 512 * w:T * ch + 512 * (w + 1)])
                                  for ch in range(NCH)])

            @block.scalar
            def _(s, w=w):
                for i, dst in enumerate((QT, KT0)):
                    trk.op("act", s, [pb[i][:]], [dst[:, 512 * w:512 * (w + 1)]],
                           lambda i=i, dst=dst, w=w: s.copy(
                               dst[:, 512 * w:512 * (w + 1)], pb[i][:]))
        for j in range(T // 128):
            @block.tensor
            def _(t, j=j):
                trk.mm_group(t, pb[2 + (j % 2)][0:128, 0:128],
                             [(S[:, T * ch + 128 * j:T * ch + 128 * (j + 1)],
                               wv_s[:, DK * ch:DK * (ch + 1)]) for ch in range(NCH)])

            @block.vector
            def _(v, j=j):
                trk.op("dve", v, [pb[2 + (j % 2)][0:128, 0:128]],
                       [V0[:, 128 * j:128 * (j + 1)]],
                       lambda j=j: v.tensor_copy(V0[:, 128 * j:128 * (j + 1)],
                                                 pb[2 + (j % 2)][0:128, 0:128]))

        def s_chunk(w):
            return lambda ch: S[:, T * ch + 512 * w:T * ch + 512 * (w + 1)]

        for w in range(NW):
            npj = 4 * w + 4
            for j in range(npj):
                pt = (pt0, pt1, pt2)[j % 3]
                bank = pb[(0, 1, 4)[j % 3]]

                @block.tensor
                def _(t, w=w, j=j, bank=bank):
                    trk.mm_group(t, bank[:],
                                 [(KT0[:, 128 * j:128 * (j + 1)],
                                   QT[:, 512 * w:512 * (w + 1)])])

                @block.scalar
                def _(s, pt=pt, bank=bank):
                    trk.op("act", s, [bank[:]], [pt],
                           lambda: s.activation(pt, bank[:], AF.Exp,
                                                scale=SCALE_QK))
                if j >= 4 * w:
                    o = j - 4 * w

                    @block.vector
                    def _(v, o=o, pt=pt):
                        trk.op("dve", v, [pt, masks[:, 512 * o:512 * (o + 1)]],
                               [pt],
                               lambda: v.tensor_tensor(
                                   pt, pt, masks[:, 512 * o:512 * (o + 1)],
                                   OP.mult))

                @block.tensor
                def _(t, j=j, pt=pt, npj=npj):
                    trk.mm(t, pb[2][:], V0[:, 128 * j:128 * (j + 1)], pt,
                           start=(j == 0), stop=(j == npj - 1))
                    trk.mm(t, pb[3][0:1, 0:512], ones_b[:], pt,
                           start=(j == 0), stop=(j == npj - 1))

            @block.vector
            def _(v):
                trk.op("dve", v, [pb[3][0:1, 0:512]], [mrow[:]],
                       lambda: v.tensor_copy(mrow[:], pb[3][0:1, 0:512]))
                trk.op("dve", v, [mrow[:]], [m2row[:]],
                       lambda: v.reciprocal(m2row[:], mrow[:]))

            @block.tensor
            def _(t):
                trk.mm_group(t, pb[4][:], [(onesr_f[:], m2row[:])])

            @block.vector
            def _(v):
                trk.op("dve", v, [pb[4][:]], [recbc[:]],
                       lambda: v.tensor_copy(recbc[:], pb[4][:]))
                trk.op("dve", v, [pb[2][:], recbc[:]], [outn],
                       lambda: v.tensor_tensor(outn, pb[2][:], recbc[:],
                                               OP.mult))

            for m in range(NCH):
                @block.tensor
                def _(t, m=m):
                    trk.mm_group(t, pb[5 + (m % 2)][:],
                                 [(wo_s[:, 128 * m:128 * (m + 1)], outn)])

                @block.vector
                def _(v, w=w, m=m):
                    sap = s_chunk(w)(m)
                    trk.op("dve", v, [pb[5 + (m % 2)][:], sap], [sap],
                           lambda sap=sap, m=m: v.tensor_tensor(
                               sap, pb[5 + (m % 2)][:], sap, OP.add))
            emit_ln_f(s_chunk(w))

        @block.tensor
        def _(t):
            trk.mm_group(t, pb[0][0:1, 0:NEXP],
                         [(S[:, T * ch:T * ch + 1],
                           wrT_s[:, NEXP * ch:NEXP * (ch + 1)])
                          for ch in range(NCH)])

        @block.vector
        def _(v):
            L = l8
            trk.op("dve", v, [pb[0][0:1, 0:NEXP]], [L[:, 0:8]],
                   lambda: v.tensor_copy(L[:, 0:8], pb[0][0:1, 0:NEXP]))
            trk.op("dve", v, [L[:, 0:8]], [L[:, 8:9]],
                   lambda: v.reduce_max(L[:, 8:9], L[:, 0:8], axis=AX.X))
            trk.op("dve", v, [L[:, 8:9]], [L[:, 9:10]],
                   lambda: v.tensor_scalar(L[:, 9:10], L[:, 8:9], -1.0, None,
                                           OP.mult))
            trk.op("dve", v, [L[:, 0:8], L[:, 8:9]], [L[:, 16:24]],
                   lambda: v.tensor_scalar(L[:, 16:24], L[:, 0:8], L[:, 8:9],
                                           None, OP.is_ge))
            trk.op("dve", v, [L[:, 16:24], L[:, 0:8]], [L[:, 24:32]],
                   lambda: v.scalar_tensor_tensor(L[:, 24:32], L[:, 16:24],
                                                  -1e30, L[:, 0:8], OP.mult,
                                                  OP.add))
            trk.op("dve", v, [L[:, 24:32]], [L[:, 10:11]],
                   lambda: v.reduce_max(L[:, 10:11], L[:, 24:32], axis=AX.X))
            trk.op("dve", v, [L[:, 0:8], L[:, 10:11]], [L[:, 16:24]],
                   lambda: v.tensor_scalar(L[:, 16:24], L[:, 0:8], L[:, 10:11],
                                           None, OP.is_ge))

        @block.scalar
        def _(s):
            trk.op("act", s, [l8[:, 0:8], l8[:, 9:10]], [l8[:, 32:40]],
                   lambda: s.activation(l8[:, 32:40], l8[:, 0:8], AF.Exp,
                                        bias=l8[:, 9:10]))

        @block.vector
        def _(v):
            L = l8
            trk.op("dve", v, [L[:, 32:40], L[:, 16:24]], [L[:, 40:48]],
                   lambda: v.tensor_tensor(L[:, 40:48], L[:, 32:40],
                                           L[:, 16:24], OP.mult))
            trk.op("dve", v, [L[:, 40:48]], [L[:, 11:12]],
                   lambda: v.reduce_sum(L[:, 11:12], L[:, 40:48], axis=AX.X))
            trk.op("dve", v, [L[:, 11:12]], [L[:, 12:13]],
                   lambda: v.reciprocal(L[:, 12:13], L[:, 11:12]))
            trk.op("dve", v, [L[:, 40:48], L[:, 12:13]], [L[:, 48:56]],
                   lambda: v.tensor_scalar(L[:, 48:56], L[:, 40:48],
                                           L[:, 12:13], None, OP.mult))

        @block.tensor
        def _(t):
            trk.mm_group(t, pb[1][0:NEXP, 0:1], [(l8[:, 48:56], one_f[:])])

        @block.vector
        def _(v):
            trk.op("dve", v, [pb[1][0:NEXP, 0:1]], [g8[:]],
                   lambda: v.tensor_copy(g8[:], pb[1][0:NEXP, 0:1]))

        for i in range(4):
            @block.tensor
            def _(t, i=i):
                trk.mm_group(t, pb[2][0:128, 0:1],
                             [(emat_s[:, 128 * i:128 * (i + 1)], g8[:])])

            @block.vector
            def _(v, i=i):
                trk.op("dve", v, [pb[2][0:128, 0:1]], [gcol[:, i:i + 1]],
                       lambda i=i: v.tensor_copy(gcol[:, i:i + 1],
                                                 pb[2][0:128, 0:1]))

        @block.vector
        def _(v):
            for i in range(4):
                trk.op("dve", v, [bstT_s[:, D * i:D * (i + 1)],
                                  gcol[:, i:i + 1]],
                       [bstT_g[:, D * i:D * (i + 1)]],
                       lambda i=i: v.tensor_scalar(
                           bstT_g[:, D * i:D * (i + 1)],
                           bstT_s[:, D * i:D * (i + 1)],
                           gcol[:, i:i + 1], None, OP.mult))

        for w in range(NW):
            for m in range(4):
                @block.tensor
                def _(t, w=w, m=m):
                    trk.mm_group(
                        t, pb[m % 2][:],
                        [(aT_s[:, 512 * ch + 128 * m:512 * ch + 128 * (m + 1)],
                          S[:, T * ch + 512 * w:T * ch + 512 * (w + 1)])
                         for ch in range(NCH)])

                @block.scalar
                def _(s, m=m):
                    trk.op("act", s, [pb[m % 2][:]],
                           [midT[:, 512 * m:512 * (m + 1)]],
                           lambda m=m: s.copy(midT[:, 512 * m:512 * (m + 1)],
                                              pb[m % 2][:]))
            for m in range(NCH):
                @block.tensor
                def _(t, m=m):
                    trk.mm_group(
                        t, pb[5 + (m % 2)][:],
                        [(bstT_g[:, D * rc + 128 * m:D * rc + 128 * (m + 1)],
                          midT[:, 512 * rc:512 * (rc + 1)]) for rc in range(4)])

                @block.vector
                def _(v, w=w, m=m):
                    sap = s_chunk(w)(m)
                    trk.op("dve", v, [pb[5 + (m % 2)][:], sap], [sap],
                           lambda sap=sap, m=m: v.scalar_tensor_tensor(
                               sap, sap, 2.0, pb[5 + (m % 2)][:], OP.mult,
                               OP.add))
            emit_ln_f(s_chunk(w))

        @block.gpsimd
        def _(g):
            _loadw(trk, g, wk1_s, wk1, NCH)
            _loadw(trk, g, wv1_s, wv1h, NCH)
            _loadw(trk, g, wq1_s, wq1, NCH)

        @block.sync
        def _(s):
            for ch in range(NCH):
                trk.dma("spd", s, x2d[128 * ch:128 * (ch + 1), :],
                        S[:, T * ch:T * (ch + 1)])

        for w in range(NWIN):
            for ch in range(NCH):
                s0 = S[:, T * ch + w:T * ch + 2048:4]
                s1 = S[:, T * ch + 2048 + w:T * ch + 4096:4]

                @block.vector
                def _(v, s0=s0, s1=s1):
                    trk.op("dve", v, [s1, s0], [tmp0[:]],
                           lambda: v.tensor_tensor(tmp0[:], s1, s0,
                                                   OP.subtract))
                    trk.op("dve", v, [tmp0[:], hf[:], s0], [tmp1[:]],
                           lambda: v.scalar_tensor_tensor(
                               tmp1[:], tmp0[:], hf[:, 0:1], s0, OP.mult,
                               OP.add))

                @block.sync
                def _(s, w=w, ch=ch):
                    trk.dma("spd", s, resGd[128 * ch:128 * (ch + 1),
                                            512 * w:512 * (w + 1)], tmp1[:])

        @block.gpsimd
        def _(g):
            trk.op("pool", g, [], [iof[:, 0:512]],
                   lambda: g.iota(iof[:, 0:512], pattern=[[1, 512]], base=0,
                                  channel_multiplier=0,
                                  allow_small_or_imprecise_dtypes=True))

        @block.scalar
        def _(s):
            trk.op("act", s, [iof[:, 0:512]], [gtq],
                   lambda: s.activation(gtq, iof[:, 0:512], AF.Exp, scale=LNG))

        @block.gpsimd
        def _(g):
            trk.op("pool", g, [], [iof[:, 0:512]],
                   lambda: g.iota(iof[:, 0:512], pattern=[[0, 4], [1, 128]],
                                  base=0, channel_multiplier=0,
                                  allow_small_or_imprecise_dtypes=True))

        @block.scalar
        def _(s):
            trk.op("act", s, [iof[:, 0:512]], [gts],
                   lambda: s.activation(gts, iof[:, 0:512], AF.Exp, scale=-LNG))

        @block.gpsimd
        def _(g):
            for o in range(4):
                trk.op("pool", g, [], [dcorr[:, 512 * o:512 * (o + 1)]],
                       lambda o=o: g.iota(dcorr[:, 512 * o:512 * (o + 1)],
                                          pattern=[[1, 512]], base=-128 * o,
                                          channel_multiplier=-1,
                                          allow_small_or_imprecise_dtypes=True))

        @block.vector
        def _(v):
            for o in range(4):
                dst = dcorr[:, 512 * o:512 * (o + 1)]
                trk.op("dve", v, [dst], [dst],
                       lambda dst=dst: v.tensor_scalar(dst, dst, 0.0, None,
                                                       OP.is_ge))
                trk.op("dve", v, [dst], [dst],
                       lambda dst=dst, o=o: v.tensor_scalar(
                           dst, dst, g_pow(-128 * o), None, OP.mult))

        for pan in range(NW):
            Spp = Span if pan % 2 == 0 else y3[:, 0:4096]

            @block.sync
            def _(s, pan=pan, Spp=Spp):
                for ch in range(NCH):
                    trk.dma("spd", s, Spp[:, 512 * ch:512 * (ch + 1)],
                            x2d[128 * ch:128 * (ch + 1),
                                512 * pan:512 * (pan + 1)])

            for m in range(NCH):
                @block.tensor
                def _(t, m=m, Spp=Spp):
                    trk.mm_group(t, pb[0 + (m % 2)][:],
                                 [(wk1_s[:, D * ch + 128 * m:D * ch + 128 * (m + 1)],
                                   Spp[:, 512 * ch:512 * (ch + 1)])
                                  for ch in range(NCH)])

                @block.vector
                def _(v, pan=pan, m=m):
                    dk = KT1[:, T * m + 512 * pan:T * m + 512 * (pan + 1)]
                    trk.op("dve", v, [pb[0 + (m % 2)][:], gts], [dk],
                           lambda dk=dk, m=m: v.tensor_tensor(
                               dk, pb[0 + (m % 2)][:], gts, OP.mult))

            for jj in range(4):
                j = 4 * pan + jj

                @block.tensor
                def _(t, jj=jj, Spp=Spp):
                    trk.mm_group(
                        t, pb[2 + (jj % 2)][:],
                        [(Spp[:, 512 * ch + 128 * jj:512 * ch + 128 * (jj + 1)],
                          wv1_s[:, 512 * ch:512 * (ch + 1)])
                         for ch in range(NCH)])

                @block.scalar
                def _(s, jj=jj, j=j):
                    trk.op("act", s, [pb[2 + (jj % 2)][:]],
                           [V1[:, 512 * j:512 * (j + 1)]],
                           lambda jj=jj, j=j: s.copy(
                               V1[:, 512 * j:512 * (j + 1)],
                               pb[2 + (jj % 2)][:]))

        for w in range(NWIN):
            @block.sync
            def _(s, w=w):
                for ch in range(NCH):
                    trk.dma("spd", s, resid[:, 512 * ch:512 * (ch + 1)],
                            resGd[128 * ch:128 * (ch + 1),
                                  512 * w:512 * (w + 1)])

            for hm in range(2):
                pan_t = 2 * w + hm

                @block.sync
                def _(s, pan_t=pan_t):
                    for ch in range(NCH):
                        trk.dma("spd", s, Span[:, 512 * ch:512 * (ch + 1)],
                                x2d[128 * ch:128 * (ch + 1),
                                    512 * pan_t:512 * (pan_t + 1)])

                for m in range(NCH):
                    @block.tensor
                    def _(t, m=m):
                        trk.mm_group(
                            t, pb[0 + (m % 2)][:],
                            [(wq1_s[:, D * ch + 128 * m:D * ch + 128 * (m + 1)],
                              Span[:, 512 * ch:512 * (ch + 1)])
                             for ch in range(NCH)])

                    @block.vector
                    def _(v, m=m):
                        dq = QTw[:, 512 * m:512 * (m + 1)]
                        trk.op("dve", v, [pb[0 + (m % 2)][:], gtq], [dq],
                               lambda dq=dq, m=m: v.tensor_tensor(
                                   dq, pb[0 + (m % 2)][:], gtq, OP.mult))

                for j in range(4 * pan_t + 4):
                    abuf = (adt0, adt1)[j % 2]
                    ab = 4 + (j % 2)

                    @block.tensor
                    def _(t, j=j, ab=ab):
                        trk.mm_group(
                            t, pb[ab][:],
                            [(KT1[:, T * ch + 128 * j:T * ch + 128 * (j + 1)],
                              QTw[:, 512 * ch:512 * (ch + 1)])
                             for ch in range(NCH)])

                    if j < 4 * pan_t:
                        sc_t = g_pow(512 * pan_t - 128 * j)

                        @block.scalar
                        def _(s, ab=ab, abuf=abuf, sc_t=sc_t):
                            trk.op("act", s, [pb[ab][:]], [abuf],
                                   lambda: s.activation(abuf, pb[ab][:],
                                                        AF.Copy, scale=sc_t))
                    else:
                        o = j - 4 * pan_t

                        @block.vector
                        def _(v, ab=ab, abuf=abuf, o=o):
                            dm = dcorr[:, 512 * o:512 * (o + 1)]
                            trk.op("dve", v, [pb[ab][:], dm], [abuf],
                                   lambda: v.tensor_tensor(abuf, pb[ab][:],
                                                           dm, OP.mult))

                    @block.tensor
                    def _(t, j=j, abuf=abuf, pan_t=pan_t):
                        last = (j == 4 * pan_t + 3)
                        for mc in range(4):
                            trk.mm(t, pb[mc][:],
                                   abuf[:, 128 * mc:128 * (mc + 1)],
                                   V1[:, 512 * j:512 * (j + 1)],
                                   start=(j == 0), stop=last)

                for mc in range(4):
                    @block.vector
                    def _(v, hm=hm, mc=mc):
                        ch = 4 * hm + mc
                        dst = y3[:, 512 * ch:512 * (ch + 1)]
                        src = resid[:, 512 * ch:512 * (ch + 1)]
                        trk.op("dve", v, [pb[mc][:], src], [dst],
                               lambda dst=dst, src=src, mc=mc: v.tensor_tensor(
                                   dst, pb[mc][:], src, OP.add))

            emit_ln_f(lambda ch: y3[:, 512 * ch:512 * (ch + 1)])

            @block.sync
            def _(s, w=w):
                for ch in range(NCH):
                    trk.dma("spd", s, x3d[128 * ch:128 * (ch + 1),
                                          512 * w:512 * (w + 1)],
                            y3[:, 512 * ch:512 * (ch + 1)])

        @block.gpsimd
        def _(g):
            _loadw(trk, g, wg_s, wgT, NCH, grp=2)
            _loadw(trk, g, wu_s, wuT, NCH, grp=2)
            _loadw(trk, g, wd_s, wdT, NH, grp=4)

        for pan in range(NPAN):
            Sp = (sp0, sp1)[pan % 2]

            @block.sync
            def _(s, pan=pan, Sp=Sp):
                for ch in range(NCH):
                    trk.dma("spd", s, Sp[:, PW * ch:PW * (ch + 1)],
                            x3d[128 * ch:128 * (ch + 1),
                                PW * pan:PW * (pan + 1)])

            def scol(ch, Sp=Sp):
                return Sp[:, PW * ch:PW * (ch + 1)]

            for hh in range(NH):
                bg = 4 + (hh % 2)
                bu = 6 + (hh % 2)

                @block.tensor
                def _(t, hh=hh, bg=bg, bu=bu, Sp=Sp):
                    trk.mm_group(t, pb[bg][0:128, 0:PW],
                                 [(wg_s[:, HID * ch + 128 * hh:HID * ch + 128 * (hh + 1)],
                                   scol(ch, Sp)) for ch in range(NCH)])
                    trk.mm_group(t, pb[bu][0:128, 0:PW],
                                 [(wu_s[:, HID * ch + 128 * hh:HID * ch + 128 * (hh + 1)],
                                   scol(ch, Sp)) for ch in range(NCH)])

                @block.scalar
                def _(s, bg=bg):
                    trk.op("act", s, [pb[bg][0:128, 0:PW]], [gs],
                           lambda: s.activation(gs, pb[bg][0:128, 0:PW],
                                                AF.Silu, scale=INV127))

                @block.vector
                def _(v, bu=bu, hh=hh):
                    trk.op("dve", v, [pb[bu][0:128, 0:PW]], [us],
                           lambda: v.tensor_scalar(us, pb[bu][0:128, 0:PW],
                                                   INV127, None, OP.mult))
                    trk.op("dve", v, [gs, us], [hs[:, PW * hh:PW * (hh + 1)]],
                           lambda hh=hh: v.tensor_tensor(
                               hs[:, PW * hh:PW * (hh + 1)], gs, us, OP.mult))

            for mc in range(NCH):
                bd = 0 + (mc % 2)

                @block.tensor
                def _(t, mc=mc, bd=bd):
                    trk.mm_group(t, pb[bd][0:128, 0:PW],
                                 [(wd_s[:, D * hh + 128 * mc:D * hh + 128 * (mc + 1)],
                                   hs[:, PW * hh:PW * (hh + 1)])
                                  for hh in range(NH)])

                @block.vector
                def _(v, mc=mc, bd=bd, Sp=Sp):
                    dst = y4[:, PW * mc:PW * (mc + 1)]
                    trk.op("dve", v, [pb[bd][0:128, 0:PW], scol(mc, Sp)], [dst],
                           lambda dst=dst, mc=mc, bd=bd, Sp=Sp:
                           v.scalar_tensor_tensor(dst, pb[bd][0:128, 0:PW],
                                                  INV127, scol(mc, Sp),
                                                  OP.mult, OP.add))
                    trk.op("dve", v, [dst], [y4b[:, PW * mc:PW * (mc + 1)]],
                           lambda dst=dst, mc=mc: v.tensor_copy(
                               y4b[:, PW * mc:PW * (mc + 1)], dst))

            mrow, m2row, varrow, rstdrow = (bufs["mrow"], bufs["m2row"],
                                            bufs["varrow"], bufs["rstdrow"])
            mbc, rbc = bufs["mbc"], bufs["rbc"]
            sq = bufs["sqbuf"]

            @block.tensor
            def _(t):
                trk.mm_group(t, pb[6][0:1, 0:PW],
                             [(ones_b[:], y4b[:, PW * ch:PW * (ch + 1)])
                              for ch in range(NCH)])

            for ch in range(NCH):
                sq2 = sq[:, 512 * (ch % 2):512 * (ch % 2) + PW]

                @block.scalar
                def _(s, ch=ch, sq2=sq2):
                    trk.op("act", s, [y4b[:, PW * ch:PW * (ch + 1)]], [sq2],
                           lambda: s.activation(
                               sq2, y4b[:, PW * ch:PW * (ch + 1)], AF.Square))

                @block.tensor
                def _(t, ch=ch, sq2=sq2):
                    trk.mm(t, pb[7][0:1, 0:PW], ones_b[:], sq2,
                           start=(ch == 0), stop=(ch == NCH - 1))

            @block.vector
            def _(v):
                trk.op("dve", v, [pb[6][0:1, 0:PW]], [mrow[:, 0:PW]],
                       lambda: v.tensor_scalar(mrow[:, 0:PW], pb[6][0:1, 0:PW],
                                               INVD, None, OP.mult))
                trk.op("dve", v, [mrow[:, 0:PW]], [m2row[:, 0:PW]],
                       lambda: v.tensor_tensor(m2row[:, 0:PW], mrow[:, 0:PW],
                                               mrow[:, 0:PW], OP.mult))
                trk.op("dve", v, [m2row[:, 0:PW]], [m2row[:, 0:PW]],
                       lambda: v.tensor_scalar(m2row[:, 0:PW], m2row[:, 0:PW],
                                               LN_EPS, None, OP.subtract))
                trk.op("dve", v, [pb[7][0:1, 0:PW], m2row[:, 0:PW]],
                       [varrow[:, 0:PW]],
                       lambda: v.scalar_tensor_tensor(
                           varrow[:, 0:PW], pb[7][0:1, 0:PW], INVD,
                           m2row[:, 0:PW], OP.mult, OP.subtract))

            @block.scalar
            def _(s):
                trk.op("act", s, [varrow[:, 0:PW]], [rstdrow[:, 0:PW]],
                       lambda: s.activation(rstdrow[:, 0:PW], varrow[:, 0:PW],
                                            AF.Ln))
                trk.op("act", s, [rstdrow[:, 0:PW]], [rstdrow[:, 0:PW]],
                       lambda: s.activation(rstdrow[:, 0:PW], rstdrow[:, 0:PW],
                                            AF.Exp, scale=-0.5))

            @block.tensor
            def _(t):
                trk.mm_group(t, pb[4][0:128, 0:PW], [(onesr_f[:], mrow[:, 0:PW])])
                trk.mm_group(t, pb[5][0:128, 0:PW],
                             [(onesr_f[:], rstdrow[:, 0:PW])])

            @block.vector
            def _(v):
                trk.op("dve", v, [pb[4][0:128, 0:PW]], [mbc[:, 0:PW]],
                       lambda: v.tensor_copy(mbc[:, 0:PW], pb[4][0:128, 0:PW]))
                trk.op("dve", v, [pb[5][0:128, 0:PW]], [rbc[:, 0:PW]],
                       lambda: v.tensor_copy(rbc[:, 0:PW], pb[5][0:128, 0:PW]))
                for ch in range(NCH):
                    dst = y4[:, PW * ch:PW * (ch + 1)]
                    trk.op("dve", v, [dst, mbc[:, 0:PW]], [dst],
                           lambda ch=ch, dst=dst: v.tensor_tensor(
                               dst, dst, mbc[:, 0:PW], OP.subtract))
                    trk.op("dve", v, [dst, rbc[:, 0:PW]], [dst],
                           lambda dst=dst: v.tensor_tensor(
                               dst, dst, rbc[:, 0:PW], OP.mult))

            @block.sync
            def _(s, pan=pan):
                for ch in range(NCH):
                    trk.dma("spd", s, x4T[128 * ch:128 * (ch + 1),
                                          PW * pan:PW * (pan + 1)],
                            y4[:, PW * ch:PW * (ch + 1)])

        @block.sync
        def _(s):
            for i in range(TRK.NDMA):
                s.wait_ge(trk.sems[f"sd{i}"], trk.cnt[f"sd{i}"])
            for i in range(TRK.NDMA):
                s.wait_ge(trk.sems[f"gd{i}"], trk.cnt[f"gd{i}"])

        with nc.Block() as rb:
            @rb.tensor
            def _(t):
                block.recs["pe"].replay(t)

            @rb.scalar
            def _(s):
                block.recs["act"].replay(s)

            @rb.vector
            def _(v):
                block.recs["dve"].replay(v)

            @rb.gpsimd
            def _(g):
                block.recs["pool"].replay(g)

            @rb.sync
            def _(s):
                block.recs["spd"].replay(s)
    return nc




_NC_CACHE = {}


def _get(name, fn):
    if name not in _NC_CACHE:
        _NC_CACHE[name] = fn()
    return _NC_CACHE[name]


def kernel(L0_attn_x, L0_attn_Wq, L0_attn_Wk, L0_attn_Wv, L0_attn_Wo,
           L0_ffn_Wr, L0_ffn_A, L0_ffn_Bw, L1_attn_att_mask, L1_attn_S_n,
           L1_attn_W_Q, L1_attn_W_K, L1_attn_W_V, L1_ffn_w_gate, L1_ffn_w_up,
           L1_ffn_w_down):
    _ = (L1_attn_att_mask, L1_attn_S_n)
    C = np.ascontiguousarray
    onescol = np.ones((128, 1), np.float32)
    onesrow = np.ones((1, 128), np.float32)
    wrT = C(np.asarray(L0_ffn_Wr).T)
    aT = C(np.asarray(L0_ffn_A).reshape(NEXP * RNK, D).T)
    bstT = C(np.transpose(np.asarray(L0_ffn_Bw), (0, 2, 1)).reshape(NEXP * RNK, D))
    emat = np.zeros((NEXP, 512), np.float32)
    for r in range(512):
        emat[r // RNK, r] = 1.0
    wgT = C(np.asarray(L1_ffn_w_gate).T)
    wuT = C(np.asarray(L1_ffn_w_up).T)
    wdT = C(np.asarray(L1_ffn_w_down).T)
    ncF = _get("F", build_F)
    in_maps = []
    for core in range(8):
        b, h = core // 2, core % 2
        in_maps.append({
            "xT": C(np.asarray(L0_attn_x)[b].T),
            "wq": np.asarray(L0_attn_Wq), "wk": np.asarray(L0_attn_Wk),
            "wv": np.asarray(L0_attn_Wv), "wo": np.asarray(L0_attn_Wo),
            "wrT": wrT, "aT": aT, "bstT": bstT, "emat": emat,
            "onescol": onescol, "onesrow": onesrow,
            "hfin": np.full((128, 1), float(h), np.float32),
            "wq1": np.asarray(L1_attn_W_Q), "wk1": np.asarray(L1_attn_W_K),
            "wv1h": C(np.asarray(L1_attn_W_V)[:, 512 * h:512 * (h + 1)]),
            "wgT": wgT, "wuT": wuT, "wdT": wdT,
        })
    res = run_bass_kernel_spmd(ncF, in_maps, core_ids=list(range(8)))

    out = np.empty((B, T, D), np.float32)
    cl = np.arange(512)
    for core in range(8):
        b, h = core // 2, core % 2
        y = np.asarray(res.results[core]["x4T"])
        for w in range(4):
            i = 2048 * h + 4 * cl + w
            out[b, i, :] = y[:, 512 * w + cl].T
    return out


# version 10 (final)
# speedup vs baseline: 2.1873x; 1.0064x over previous
import contextlib
import math
import numpy as np
import ml_dtypes
import concourse.bass as bass
import concourse.mybir as mybir
from concourse.bass_utils import run_bass_kernel_spmd
from concourse.alu_op_type import AluOpType

F32 = mybir.dt.float32
BF16 = mybir.dt.bfloat16
I32 = mybir.dt.int32
AF = mybir.ActivationFunctionType
AX = mybir.AxisListType
BFDT = ml_dtypes.bfloat16
OP = AluOpType

B, T, D = 4, 4096, 1024
DK = 128
NEXP, RNK = 8, 64
HID = 3072
LN_EPS = 1e-5
SCALE_QK = 128.0 ** -0.5
INV127 = 1.0 / 127.0
INVD = 1.0 / 1024.0
LNG = math.log(0.99)
NCH = D // 128
NW = T // 512


def g_pow(k):
    return math.exp(LNG * k)



def _box(ap):
    t = ap.tensor
    name = t.name
    shape = list(t.shape)
    kind = type(t).__name__
    ent = [list(x) for x in ap.ap]
    off = int(ap.offset)
    if kind == "DRamTensorHandle":
        hi = off
        for st, cn in ent:
            hi += abs(int(st)) * (int(cn) - 1)
        return (name, 0, 1, off, hi + 1)
    fsz = 1
    for s in shape[1:]:
        fsz *= s
    p0 = off // fsz
    c0 = off % fsz
    pstep, pcnt = int(ent[0][0]), int(ent[0][1])
    p1 = p0 + (pcnt - 1) * (pstep // fsz if pcnt > 1 else 1) + 1
    cspan = 0
    for stp, cn in ent[1:]:
        cspan += abs(int(stp)) * (int(cn) - 1)
    return (name, p0, p1, c0, c0 + cspan + 1)


def _overlap(a, b):
    return (a[0] == b[0] and a[1] < b[2] and b[1] < a[2]
            and a[3] < b[4] and b[3] < a[4])


def _covers(a, b):
    return (a[0] == b[0] and a[1] <= b[1] and a[2] >= b[2] and a[3] <= b[3]
            and a[4] >= b[4])


class TRK:

    ENG = {"pe": "tk", "act": "sc", "dve": "vc", "pool": "gc"}
    NDMA = 8

    def __init__(self, nc, stack):
        self.nc = nc
        self.sems = {}
        self.cnt = {}
        for s in self.ENG.values():
            self.sems[s] = stack.enter_context(nc.semaphore(s))
            self.cnt[s] = 0
        for q in ("sd", "gd"):
            for i in range(self.NDMA):
                s = f"{q}{i}"
                self.sems[s] = stack.enter_context(nc.semaphore(s))
                self.cnt[s] = 0
        self.dma_i = {"spd": 0, "gpd": 0}
        self.known = {e: {} for e in ("pe", "act", "dve", "pool", "spd", "gpd")}
        self.writes = {}
        self.reads = {}
        self.grp = {}

    def _edges(self, box, is_write, need):
        nm = box[0]
        for b, s, c in self.writes.get(nm, ()):
            if _overlap(b, box):
                need[s] = max(need.get(s, 0), c)
        if is_write:
            for b, s, c in self.reads.get(nm, ()):
                if _overlap(b, box):
                    need[s] = max(need.get(s, 0), c)

    def _record(self, box, is_write, sem, cnt):
        nm = box[0]
        if is_write:
            self.writes.setdefault(nm, [])
            self.writes[nm] = [x for x in self.writes[nm] if not _covers(box, x[0])]
            self.writes[nm].append((box, sem, cnt))
            self.reads.setdefault(nm, [])
            self.reads[nm] = [x for x in self.reads[nm] if not _covers(box, x[0])]
        else:
            self.reads.setdefault(nm, []).append((box, sem, cnt))

    def _wait(self, eng_name, eng, need):
        kn = self.known[eng_name]
        for s, v in sorted(need.items()):
            if eng_name == "pe" and s == "tk":
                continue
            if kn.get(s, 0) >= v:
                continue
            eng.wait_ge(self.sems[s], v)
            kn[s] = v

    def op(self, eng_name, eng, reads, writes, fn, inc=None):
        is_dma = eng_name in ("spd", "gpd")
        if is_dma:
            i = self.dma_i[eng_name]
            self.dma_i[eng_name] = i + 1
            mysem = f"{'sd' if eng_name == 'spd' else 'gd'}{i % self.NDMA}"
            inc = 16
        else:
            mysem = self.ENG[eng_name]
            inc = 1 if inc is None else inc
        rb = [_box(a) for a in reads]
        wb = [_box(a) for a in writes]
        need = {}
        for b in rb:
            self._edges(b, False, need)
        for b in wb:
            self._edges(b, True, need)
        if is_dma:
            need[mysem] = max(need.get(mysem, 0), self.cnt[mysem])
        self._wait(eng_name, eng, need)
        inst = fn()
        self.cnt[mysem] += inc
        cnt = self.cnt[mysem]
        inst.then_inc(self.sems[mysem], inc)
        for b in rb:
            self._record(b, False, mysem, cnt)
        for b in wb:
            self._record(b, True, mysem, cnt)
        return cnt

    def dma(self, eng_name, eng, out, in_):
        return self.op(eng_name, eng, [in_], [out],
                       lambda: eng.dma_start(out=out, in_=in_))

    def mm(self, t, out, lhsT, rhs, start, stop):
        rb = [_box(lhsT), _box(rhs)]
        wbox = _box(out)
        need = {}
        for b in rb:
            self._edges(b, False, need)
        self._edges(wbox, True, need)
        self._wait("pe", t, need)
        inst = t.matmul(out, lhsT, rhs, start=start, stop=stop)
        self.cnt["tk"] += 1
        cnt = self.cnt["tk"]
        inst.then_inc(self.sems["tk"], 1)
        for b in rb:
            self._record(b, False, "tk", cnt)
        self._record(wbox, True, "tk", cnt)
        return inst

    def mm_group(self, t, out, pairs):
        n = len(pairs)
        for i, (lhsT, rhs) in enumerate(pairs):
            self.mm(t, out, lhsT, rhs, start=(i == 0), stop=(i == n - 1))
        return self.cnt["tk"]


class _RecInst:
    def __init__(self):
        self.incs = []

    def then_inc(self, sem, k):
        self.incs.append((sem, k))
        return self


class _Rec:

    def __init__(self):
        self.calls = []

    def __getattr__(self, name):
        def call(*a, **kw):
            ri = _RecInst()
            self.calls.append((name, a, kw, ri))
            return ri
        return call

    def replay(self, eng):
        for name, a, kw, ri in self.calls:
            inst = getattr(eng, name)(*a, **kw)
            for sem, k in ri.incs:
                inst.then_inc(sem, k)


class _FakeBlock:

    def __init__(self):
        self.recs = {k: _Rec() for k in ("pe", "act", "dve", "pool", "spd")}

    def tensor(self, f):
        f(self.recs["pe"])

    def scalar(self, f):
        f(self.recs["act"])

    def vector(self, f):
        f(self.recs["dve"])

    def gpsimd(self, f):
        f(self.recs["pool"])

    def sync(self, f):
        f(self.recs["spd"])



def emit_ln(trk, block, bufs, s_ap, width=512, nch=NCH):
    pb = bufs["pb"]
    sq, mrow, m2row, varrow, rstdrow = (bufs["sqbuf"], bufs["mrow"], bufs["m2row"],
                                        bufs["varrow"], bufs["rstdrow"])
    mbc, rbc, tmp0, tmp1 = bufs["mbc"], bufs["rbc"], bufs["tmp0"], bufs["tmp1"]
    onesb, onesrf = bufs["ones_b"], bufs["onesr_f"]
    b_sy, b_sy2, b_m, b_r = bufs["ln_banks"]
    W = width

    @block.scalar
    def _(s):
        for ch in range(nch):
            trk.op("act", s, [s_ap(ch)], [sq[:, W * ch:W * (ch + 1)]],
                   lambda ch=ch: s.activation(sq[:, W * ch:W * (ch + 1)],
                                              s_ap(ch), AF.Square))

    @block.tensor
    def _(t):
        trk.mm_group(t, pb[b_sy][0:1, 0:W],
                     [(onesb[:], s_ap(ch)) for ch in range(nch)])
        trk.mm_group(t, pb[b_sy2][0:1, 0:W],
                     [(onesb[:], sq[:, W * ch:W * (ch + 1)]) for ch in range(nch)])

    @block.vector
    def _(v):
        trk.op("dve", v, [pb[b_sy][0:1, 0:W]], [mrow[:, 0:W]],
               lambda: v.tensor_scalar(mrow[:, 0:W], pb[b_sy][0:1, 0:W], INVD,
                                       None, OP.mult))
        trk.op("dve", v, [mrow[:, 0:W]], [m2row[:, 0:W]],
               lambda: v.tensor_tensor(m2row[:, 0:W], mrow[:, 0:W], mrow[:, 0:W],
                                       OP.mult))
        trk.op("dve", v, [m2row[:, 0:W]], [m2row[:, 0:W]],
               lambda: v.tensor_scalar(m2row[:, 0:W], m2row[:, 0:W], LN_EPS, None,
                                       OP.subtract))
        trk.op("dve", v, [pb[b_sy2][0:1, 0:W], m2row[:, 0:W]], [varrow[:, 0:W]],
               lambda: v.scalar_tensor_tensor(varrow[:, 0:W], pb[b_sy2][0:1, 0:W],
                                              INVD, m2row[:, 0:W], OP.mult,
                                              OP.subtract))

    @block.scalar
    def _(s):
        trk.op("act", s, [varrow[:, 0:W]], [rstdrow[:, 0:W]],
               lambda: s.activation(rstdrow[:, 0:W], varrow[:, 0:W], AF.Ln))
        trk.op("act", s, [rstdrow[:, 0:W]], [rstdrow[:, 0:W]],
               lambda: s.activation(rstdrow[:, 0:W], rstdrow[:, 0:W], AF.Exp,
                                    scale=-0.5))

    @block.tensor
    def _(t):
        trk.mm_group(t, pb[b_m][0:128, 0:W], [(onesrf[:], mrow[:, 0:W])])
        trk.mm_group(t, pb[b_r][0:128, 0:W], [(onesrf[:], rstdrow[:, 0:W])])

    done = []

    @block.vector
    def _(v):
        trk.op("dve", v, [pb[b_m][0:128, 0:W]], [mbc[:, 0:W]],
               lambda: v.tensor_copy(mbc[:, 0:W], pb[b_m][0:128, 0:W]))
        trk.op("dve", v, [pb[b_r][0:128, 0:W]], [rbc[:, 0:W]],
               lambda: v.tensor_copy(rbc[:, 0:W], pb[b_r][0:128, 0:W]))
        for ch in range(nch):
            tmp = (tmp0, tmp1)[ch % 2]
            trk.op("dve", v, [s_ap(ch), mbc[:, 0:W]], [tmp[:, 0:W]],
                   lambda ch=ch, tmp=tmp: v.tensor_tensor(tmp[:, 0:W], s_ap(ch),
                                                          mbc[:, 0:W], OP.subtract))
            done.append(trk.op("dve", v, [tmp[:, 0:W], rbc[:, 0:W]], [s_ap(ch)],
                               lambda ch=ch, tmp=tmp: v.tensor_tensor(
                                   s_ap(ch), tmp[:, 0:W], rbc[:, 0:W], OP.mult)))
    return done


def _loadw(trk, g, dst, src, nch, grp=4):
    if nch == 1:
        return trk.dma("gpd", g, dst[:], src)
    fw = src.tensor.shape[1]
    for c0 in range(0, nch, grp):
        n = min(grp, nch - c0)
        trk.dma("gpd", g,
                dst[:, fw * c0:fw * (c0 + n)].rearrange("p (c f) -> p c f", c=n),
                src[128 * c0:128 * (c0 + n), :].rearrange("(c p) f -> p c f", c=n))


def common_bufs(nc, e):
    bufs = {}
    bufs["sqbuf"] = e(nc.sbuf_tensor("sqbuf", [128, NCH * 512], BF16))
    for nm in ("mrow", "m2row", "varrow", "rstdrow"):
        bufs[nm] = e(nc.sbuf_tensor(nm, [1, 512], F32))
    bufs["mbc"] = e(nc.sbuf_tensor("mbc", [128, 512], F32))
    bufs["rbc"] = e(nc.sbuf_tensor("rbc", [128, 512], F32))
    bufs["tmp0"] = e(nc.sbuf_tensor("tmp0", [128, 512], BF16))
    bufs["tmp1"] = e(nc.sbuf_tensor("tmp1", [128, 512], BF16))
    bufs["ones_b"] = e(nc.sbuf_tensor("ones_b", [128, 1], BF16))
    bufs["onesr_f"] = e(nc.sbuf_tensor("onesr_f", [1, 128], F32))
    return bufs



def build_A():
    nc = bass.Bass("TRN2", target_bir_lowering=False)
    xT = nc.dram_tensor("xT", [D, T], F32, kind="ExternalInput").ap()
    wq = nc.dram_tensor("wq", [D, DK], F32, kind="ExternalInput").ap()
    wk = nc.dram_tensor("wk", [D, DK], F32, kind="ExternalInput").ap()
    wv = nc.dram_tensor("wv", [D, DK], F32, kind="ExternalInput").ap()
    wo = nc.dram_tensor("wo", [DK, D], F32, kind="ExternalInput").ap()
    wrT = nc.dram_tensor("wrT", [D, NEXP], F32, kind="ExternalInput").ap()
    aT = nc.dram_tensor("aT", [D, NEXP * RNK], F32, kind="ExternalInput").ap()
    bstT = nc.dram_tensor("bstT", [NEXP * RNK, D], F32, kind="ExternalInput").ap()
    emat = nc.dram_tensor("emat", [NEXP, 512], F32, kind="ExternalInput").ap()
    onescol = nc.dram_tensor("onescol", [128, 1], F32, kind="ExternalInput").ap()
    onesrow = nc.dram_tensor("onesrow", [1, 128], F32, kind="ExternalInput").ap()
    x2T = nc.dram_tensor("x2T", [D, T], BF16, kind="ExternalOutput").ap()

    with contextlib.ExitStack() as st_:
        e = st_.enter_context
        S = e(nc.sbuf_tensor("S", [128, NCH * T], BF16))
        QT = e(nc.sbuf_tensor("QT", [128, T], BF16))
        KT = e(nc.sbuf_tensor("KT", [128, T], BF16))
        V = e(nc.sbuf_tensor("V", [128, T], BF16))
        wq_s = e(nc.sbuf_tensor("wq_s", [128, NCH * DK], BF16))
        wk_s = e(nc.sbuf_tensor("wk_s", [128, NCH * DK], BF16))
        wv_s = e(nc.sbuf_tensor("wv_s", [128, NCH * DK], BF16))
        wo_s = e(nc.sbuf_tensor("wo_s", [128, D], BF16))
        wrT_s = e(nc.sbuf_tensor("wrT_s", [128, NCH * NEXP], BF16))
        aT_s = e(nc.sbuf_tensor("aT_s", [128, NCH * 512], BF16))
        bstT_s = e(nc.sbuf_tensor("bstT_s", [128, 4 * D], BF16))
        bstT_g = e(nc.sbuf_tensor("bstT_g", [128, 4 * D], BF16))
        emat_s = e(nc.sbuf_tensor("emat_s", [NEXP, 512], F32))
        one_f = e(nc.sbuf_tensor("one_f", [1, 1], F32))
        masks = e(nc.sbuf_tensor("masks", [128, 4 * 512], BF16))
        iotbuf = e(nc.sbuf_tensor("iotbuf", [128, 4 * 512], F32))
        pt0 = e(nc.sbuf_tensor("pt0", [128, 512], BF16))
        pt1 = e(nc.sbuf_tensor("pt1", [128, 512], BF16))
        outn = e(nc.sbuf_tensor("outn", [128, 512], BF16))
        sumrow = e(nc.sbuf_tensor("sumrow", [1, 512], F32))
        recrow = e(nc.sbuf_tensor("recrow", [1, 512], F32))
        recbc = e(nc.sbuf_tensor("recbc", [128, 512], F32))
        midT = e(nc.sbuf_tensor("midT", [128, 4 * 512], BF16))
        l8 = e(nc.sbuf_tensor("l8", [1, 64], F32))
        g8 = e(nc.sbuf_tensor("g8", [NEXP, 1], F32))
        gcol = e(nc.sbuf_tensor("gcol", [128, 4], F32))
        bufs = common_bufs(nc, e)
        pb = [e(nc.psum_tensor(f"pb{i}", [128, 512], F32)) for i in range(8)]
        bufs["pb"] = pb
        bufs["ln_banks"] = (7, 4, 5, 6)
        onesb, onesrf = bufs["ones_b"], bufs["onesr_f"]

        trk = TRK(nc, st_)
        block = e(nc.Block())

        @block.gpsimd
        def _(g):
            for ch in range(NCH):
                trk.dma("gpd", g, S[:, T * ch:T * (ch + 1)],
                        xT[128 * ch:128 * (ch + 1), :])
            _loadw(trk, g, wq_s, wq, NCH)
            _loadw(trk, g, wk_s, wk, NCH)
            _loadw(trk, g, wv_s, wv, NCH)
            _loadw(trk, g, wo_s, wo, 1)
            _loadw(trk, g, wrT_s, wrT, NCH)
            _loadw(trk, g, aT_s, aT, NCH)
            _loadw(trk, g, bstT_s, bstT, 4)
            trk.dma("gpd", g, onesb[:], onescol)
            for o in range(4):
                trk.op("pool", g, [], [iotbuf[:, 512 * o:512 * (o + 1)]],
                       lambda o=o: g.iota(iotbuf[:, 512 * o:512 * (o + 1)],
                                          pattern=[[1, 512]], base=-128 * o,
                                          channel_multiplier=-1,
                                          allow_small_or_imprecise_dtypes=True))

        @block.sync
        def _(s):
            trk.dma("spd", s, emat_s[:], emat)
            trk.dma("spd", s, onesrf[:], onesrow)
            trk.dma("spd", s, one_f[:], onesrow[:, 0:1])

        @block.vector
        def _(v):
            trk.op("dve", v, [iotbuf[:]], [masks[:]],
                   lambda: v.tensor_scalar(masks[:], iotbuf[:], 0.0, None,
                                           OP.is_ge))

        for w in range(NW):
            @block.tensor
            def _(t, w=w):
                for i, wsb in enumerate((wq_s, wk_s)):
                    trk.mm_group(t, pb[i][:],
                                 [(wsb[:, DK * ch:DK * (ch + 1)],
                                   S[:, T * ch + 512 * w:T * ch + 512 * (w + 1)])
                                  for ch in range(NCH)])

            @block.scalar
            def _(s, w=w):
                for i, dst in enumerate((QT, KT)):
                    trk.op("act", s, [pb[i][:]], [dst[:, 512 * w:512 * (w + 1)]],
                           lambda i=i, dst=dst, w=w: s.copy(
                               dst[:, 512 * w:512 * (w + 1)], pb[i][:]))
        for j in range(T // 128):
            @block.tensor
            def _(t, j=j):
                trk.mm_group(t, pb[2 + (j % 2)][0:128, 0:128],
                             [(S[:, T * ch + 128 * j:T * ch + 128 * (j + 1)],
                               wv_s[:, DK * ch:DK * (ch + 1)]) for ch in range(NCH)])

            @block.vector
            def _(v, j=j):
                trk.op("dve", v, [pb[2 + (j % 2)][0:128, 0:128]],
                       [V[:, 128 * j:128 * (j + 1)]],
                       lambda j=j: v.tensor_copy(V[:, 128 * j:128 * (j + 1)],
                                                 pb[2 + (j % 2)][0:128, 0:128]))

        def s_chunk(w):
            return lambda ch: S[:, T * ch + 512 * w:T * ch + 512 * (w + 1)]

        for w in range(NW):
            npj = 4 * w + 4
            for j in range(npj):
                pt = (pt0, pt1)[j % 2]
                bank = pb[j % 2]

                @block.tensor
                def _(t, w=w, j=j, bank=bank):
                    trk.mm_group(t, bank[:],
                                 [(KT[:, 128 * j:128 * (j + 1)],
                                   QT[:, 512 * w:512 * (w + 1)])])

                @block.scalar
                def _(s, pt=pt, bank=bank):
                    trk.op("act", s, [bank[:]], [pt[:]],
                           lambda: s.activation(pt[:], bank[:], AF.Exp,
                                                scale=SCALE_QK))
                if j >= 4 * w:
                    o = j - 4 * w

                    @block.vector
                    def _(v, o=o, pt=pt):
                        trk.op("dve", v, [pt[:], masks[:, 512 * o:512 * (o + 1)]],
                               [pt[:]],
                               lambda: v.tensor_tensor(
                                   pt[:], pt[:], masks[:, 512 * o:512 * (o + 1)],
                                   OP.mult))

                @block.tensor
                def _(t, j=j, pt=pt, npj=npj):
                    trk.mm(t, pb[2][:], V[:, 128 * j:128 * (j + 1)], pt[:],
                           start=(j == 0), stop=(j == npj - 1))
                    trk.mm(t, pb[3][0:1, 0:512], onesb[:], pt[:],
                           start=(j == 0), stop=(j == npj - 1))

            @block.vector
            def _(v):
                trk.op("dve", v, [pb[3][0:1, 0:512]], [sumrow[:]],
                       lambda: v.tensor_copy(sumrow[:], pb[3][0:1, 0:512]))
                trk.op("dve", v, [sumrow[:]], [recrow[:]],
                       lambda: v.reciprocal(recrow[:], sumrow[:]))

            @block.tensor
            def _(t):
                trk.mm_group(t, pb[4][:], [(onesrf[:], recrow[:])])

            @block.vector
            def _(v):
                trk.op("dve", v, [pb[4][:]], [recbc[:]],
                       lambda: v.tensor_copy(recbc[:], pb[4][:]))
                trk.op("dve", v, [pb[2][:], recbc[:]], [outn[:]],
                       lambda: v.tensor_tensor(outn[:], pb[2][:], recbc[:],
                                               OP.mult))

            for m in range(NCH):
                @block.tensor
                def _(t, m=m):
                    trk.mm_group(t, pb[5 + (m % 2)][:],
                                 [(wo_s[:, 128 * m:128 * (m + 1)], outn[:])])

                @block.vector
                def _(v, w=w, m=m):
                    sap = s_chunk(w)(m)
                    trk.op("dve", v, [pb[5 + (m % 2)][:], sap], [sap],
                           lambda sap=sap, m=m: v.tensor_tensor(
                               sap, pb[5 + (m % 2)][:], sap, OP.add))
            emit_ln(trk, block, bufs, s_chunk(w))

        @block.tensor
        def _(t):
            trk.mm_group(t, pb[0][0:1, 0:NEXP],
                         [(S[:, T * ch:T * ch + 1],
                           wrT_s[:, NEXP * ch:NEXP * (ch + 1)])
                          for ch in range(NCH)])

        @block.vector
        def _(v):
            L = l8
            trk.op("dve", v, [pb[0][0:1, 0:NEXP]], [L[:, 0:8]],
                   lambda: v.tensor_copy(L[:, 0:8], pb[0][0:1, 0:NEXP]))
            trk.op("dve", v, [L[:, 0:8]], [L[:, 8:9]],
                   lambda: v.reduce_max(L[:, 8:9], L[:, 0:8], axis=AX.X))
            trk.op("dve", v, [L[:, 8:9]], [L[:, 9:10]],
                   lambda: v.tensor_scalar(L[:, 9:10], L[:, 8:9], -1.0, None,
                                           OP.mult))
            trk.op("dve", v, [L[:, 0:8], L[:, 8:9]], [L[:, 16:24]],
                   lambda: v.tensor_scalar(L[:, 16:24], L[:, 0:8], L[:, 8:9],
                                           None, OP.is_ge))
            trk.op("dve", v, [L[:, 16:24], L[:, 0:8]], [L[:, 24:32]],
                   lambda: v.scalar_tensor_tensor(L[:, 24:32], L[:, 16:24], -1e30,
                                                  L[:, 0:8], OP.mult, OP.add))
            trk.op("dve", v, [L[:, 24:32]], [L[:, 10:11]],
                   lambda: v.reduce_max(L[:, 10:11], L[:, 24:32], axis=AX.X))
            trk.op("dve", v, [L[:, 0:8], L[:, 10:11]], [L[:, 16:24]],
                   lambda: v.tensor_scalar(L[:, 16:24], L[:, 0:8], L[:, 10:11],
                                           None, OP.is_ge))

        @block.scalar
        def _(s):
            trk.op("act", s, [l8[:, 0:8], l8[:, 9:10]], [l8[:, 32:40]],
                   lambda: s.activation(l8[:, 32:40], l8[:, 0:8], AF.Exp,
                                        bias=l8[:, 9:10]))

        @block.vector
        def _(v):
            L = l8
            trk.op("dve", v, [L[:, 32:40], L[:, 16:24]], [L[:, 40:48]],
                   lambda: v.tensor_tensor(L[:, 40:48], L[:, 32:40], L[:, 16:24],
                                           OP.mult))
            trk.op("dve", v, [L[:, 40:48]], [L[:, 11:12]],
                   lambda: v.reduce_sum(L[:, 11:12], L[:, 40:48], axis=AX.X))
            trk.op("dve", v, [L[:, 11:12]], [L[:, 12:13]],
                   lambda: v.reciprocal(L[:, 12:13], L[:, 11:12]))
            trk.op("dve", v, [L[:, 40:48], L[:, 12:13]], [L[:, 48:56]],
                   lambda: v.tensor_scalar(L[:, 48:56], L[:, 40:48], L[:, 12:13],
                                           None, OP.mult))

        @block.tensor
        def _(t):
            trk.mm_group(t, pb[1][0:NEXP, 0:1], [(l8[:, 48:56], one_f[:])])

        @block.vector
        def _(v):
            trk.op("dve", v, [pb[1][0:NEXP, 0:1]], [g8[:]],
                   lambda: v.tensor_copy(g8[:], pb[1][0:NEXP, 0:1]))

        for i in range(4):
            @block.tensor
            def _(t, i=i):
                trk.mm_group(t, pb[2][0:128, 0:1],
                             [(emat_s[:, 128 * i:128 * (i + 1)], g8[:])])

            @block.vector
            def _(v, i=i):
                trk.op("dve", v, [pb[2][0:128, 0:1]], [gcol[:, i:i + 1]],
                       lambda i=i: v.tensor_copy(gcol[:, i:i + 1],
                                                 pb[2][0:128, 0:1]))

        @block.vector
        def _(v):
            for i in range(4):
                trk.op("dve", v, [bstT_s[:, D * i:D * (i + 1)], gcol[:, i:i + 1]],
                       [bstT_g[:, D * i:D * (i + 1)]],
                       lambda i=i: v.tensor_scalar(bstT_g[:, D * i:D * (i + 1)],
                                                   bstT_s[:, D * i:D * (i + 1)],
                                                   gcol[:, i:i + 1], None,
                                                   OP.mult))

        for w in range(NW):
            for m in range(4):
                @block.tensor
                def _(t, w=w, m=m):
                    trk.mm_group(
                        t, pb[m % 2][:],
                        [(aT_s[:, 512 * ch + 128 * m:512 * ch + 128 * (m + 1)],
                          S[:, T * ch + 512 * w:T * ch + 512 * (w + 1)])
                         for ch in range(NCH)])

                @block.scalar
                def _(s, m=m):
                    trk.op("act", s, [pb[m % 2][:]],
                           [midT[:, 512 * m:512 * (m + 1)]],
                           lambda m=m: s.copy(midT[:, 512 * m:512 * (m + 1)],
                                              pb[m % 2][:]))
            for m in range(NCH):
                @block.tensor
                def _(t, m=m):
                    trk.mm_group(
                        t, pb[5 + (m % 2)][:],
                        [(bstT_g[:, D * rc + 128 * m:D * rc + 128 * (m + 1)],
                          midT[:, 512 * rc:512 * (rc + 1)]) for rc in range(4)])

                @block.vector
                def _(v, w=w, m=m):
                    sap = s_chunk(w)(m)
                    trk.op("dve", v, [pb[5 + (m % 2)][:], sap], [sap],
                           lambda sap=sap, m=m: v.scalar_tensor_tensor(
                               sap, sap, 2.0, pb[5 + (m % 2)][:], OP.mult,
                               OP.add))
            emit_ln(trk, block, bufs, s_chunk(w))

        @block.sync
        def _(s):
            for ch in range(NCH):
                trk.dma("spd", s, x2T[128 * ch:128 * (ch + 1), :],
                        S[:, T * ch:T * (ch + 1)])
            for i in range(TRK.NDMA):
                s.wait_ge(trk.sems[f"sd{i}"], trk.cnt[f"sd{i}"])
    return nc



def build_B():
    nc = bass.Bass("TRN2", target_bir_lowering=False)
    x2Td = nc.dram_tensor("x2T", [D, T], BF16, kind="ExternalInput").ap()
    resG = nc.dram_tensor("resG", [D, 2048], BF16, kind="ExternalInput").ap()
    wq1 = nc.dram_tensor("wq1", [D, D], F32, kind="ExternalInput").ap()
    wk1 = nc.dram_tensor("wk1", [D, D], F32, kind="ExternalInput").ap()
    wv1h = nc.dram_tensor("wv1h", [D, 512], F32, kind="ExternalInput").ap()
    onescol = nc.dram_tensor("onescol", [128, 1], F32, kind="ExternalInput").ap()
    onesrow = nc.dram_tensor("onesrow", [1, 128], F32, kind="ExternalInput").ap()
    x3gT = nc.dram_tensor("x3gT", [D, 2048], BF16, kind="ExternalOutput").ap()

    NWIN = 4
    with contextlib.ExitStack() as st_:
        e = st_.enter_context
        KT = e(nc.sbuf_tensor("KT", [128, NCH * T], BF16))
        V = e(nc.sbuf_tensor("V", [128, 32 * 512], BF16))
        QTw = e(nc.sbuf_tensor("QTw", [128, NCH * 512], BF16))
        Span0 = e(nc.sbuf_tensor("Span0", [128, NCH * 512], BF16))
        wq1_s = e(nc.sbuf_tensor("wq1_s", [128, NCH * D], BF16))
        wk1_s = e(nc.sbuf_tensor("wk1_s", [128, NCH * D], BF16))
        wv1_s = e(nc.sbuf_tensor("wv1_s", [128, NCH * 512], BF16))
        adt0 = e(nc.sbuf_tensor("adt0", [128, 512], BF16))
        adt1 = e(nc.sbuf_tensor("adt1", [128, 512], BF16))
        gtq = e(nc.sbuf_tensor("gtq", [128, 512], BF16))
        gts = e(nc.sbuf_tensor("gts", [128, 512], BF16))
        dcorr = e(nc.sbuf_tensor("dcorr", [128, 4 * 512], F32))
        iotb = e(nc.sbuf_tensor("iotb", [128, 2 * 512], F32))
        resid = e(nc.sbuf_tensor("resid", [128, NCH * 512], BF16))
        y3 = e(nc.sbuf_tensor("y3", [128, NCH * 512], BF16))
        bufs = common_bufs(nc, e)
        pb = [e(nc.psum_tensor(f"pb{i}", [128, 512], F32)) for i in range(8)]
        bufs["pb"] = pb
        bufs["ln_banks"] = (7, 4, 5, 6)
        onesb, onesrf = bufs["ones_b"], bufs["onesr_f"]

        trk = TRK(nc, st_)
        block = e(nc.Block())

        @block.gpsimd
        def _(g):
            trk.dma("gpd", g, onesb[:], onescol)
            trk.op("pool", g, [], [iotb[:, 0:512]],
                   lambda: g.iota(iotb[:, 0:512], pattern=[[1, 512]], base=0,
                                  channel_multiplier=0,
                                  allow_small_or_imprecise_dtypes=True))
            trk.op("pool", g, [], [iotb[:, 512:1024]],
                   lambda: g.iota(iotb[:, 512:1024], pattern=[[0, 4], [1, 128]],
                                  base=0, channel_multiplier=0,
                                  allow_small_or_imprecise_dtypes=True))
            for o in range(4):
                trk.op("pool", g, [], [dcorr[:, 512 * o:512 * (o + 1)]],
                       lambda o=o: g.iota(
                           dcorr[:, 512 * o:512 * (o + 1)],
                           pattern=[[1, 512]], base=-128 * o,
                           channel_multiplier=-1,
                           allow_small_or_imprecise_dtypes=True))
            _loadw(trk, g, wq1_s, wq1, NCH)
            _loadw(trk, g, wk1_s, wk1, NCH)
            _loadw(trk, g, wv1_s, wv1h, NCH)

        @block.sync
        def _(s):
            trk.dma("spd", s, onesrf[:], onesrow)

        @block.scalar
        def _(s):
            trk.op("act", s, [iotb[:, 0:512]], [gtq[:]],
                   lambda: s.activation(gtq[:], iotb[:, 0:512], AF.Exp, scale=LNG))
            trk.op("act", s, [iotb[:, 512:1024]], [gts[:]],
                   lambda: s.activation(gts[:], iotb[:, 512:1024], AF.Exp,
                                        scale=-LNG))

        @block.vector
        def _(v):
            for o in range(4):
                dst = dcorr[:, 512 * o:512 * (o + 1)]
                trk.op("dve", v, [dst], [dst],
                       lambda dst=dst: v.tensor_scalar(
                           dst, dst, 0.0, None, OP.is_ge))
                trk.op("dve", v, [dst], [dst],
                       lambda dst=dst, o=o: v.tensor_scalar(
                           dst, dst, g_pow(-128 * o), None, OP.mult))

        for pan in range(NW):
            Sp = Span0

            @block.sync
            def _(s, pan=pan, Sp=Sp):
                for ch in range(NCH):
                    trk.dma("spd", s, Sp[:, 512 * ch:512 * (ch + 1)],
                            x2Td[128 * ch:128 * (ch + 1),
                                 512 * pan:512 * (pan + 1)])

            for m in range(NCH):
                @block.tensor
                def _(t, Sp=Sp, m=m):
                    trk.mm_group(t, pb[0 + (m % 2)][:],
                                 [(wk1_s[:, D * ch + 128 * m:D * ch + 128 * (m + 1)],
                                   Sp[:, 512 * ch:512 * (ch + 1)])
                                  for ch in range(NCH)])

                @block.vector
                def _(v, pan=pan, m=m):
                    dk = KT[:, T * m + 512 * pan:T * m + 512 * (pan + 1)]
                    trk.op("dve", v, [pb[0 + (m % 2)][:], gts[:]], [dk],
                           lambda dk=dk, m=m: v.tensor_tensor(
                               dk, pb[0 + (m % 2)][:], gts[:], OP.mult))

            for jj in range(4):
                j = 4 * pan + jj

                @block.tensor
                def _(t, Sp=Sp, jj=jj):
                    trk.mm_group(
                        t, pb[2 + (jj % 2)][:],
                        [(Sp[:, 512 * ch + 128 * jj:512 * ch + 128 * (jj + 1)],
                          wv1_s[:, 512 * ch:512 * (ch + 1)]) for ch in range(NCH)])

                @block.scalar
                def _(s, jj=jj, j=j):
                    trk.op("act", s, [pb[2 + (jj % 2)][:]],
                           [V[:, 512 * j:512 * (j + 1)]],
                           lambda jj=jj, j=j: s.copy(V[:, 512 * j:512 * (j + 1)],
                                                     pb[2 + (jj % 2)][:]))

        for w in range(NWIN):
            @block.sync
            def _(s, w=w):
                for ch in range(NCH):
                    trk.dma("spd", s, resid[:, 512 * ch:512 * (ch + 1)],
                            resG[128 * ch:128 * (ch + 1), 512 * w:512 * (w + 1)])

            for hm in range(2):
                pan_t = 2 * w + hm
                @block.sync
                def _(s, pan_t=pan_t):
                    for ch in range(NCH):
                        trk.dma("spd", s, Span0[:, 512 * ch:512 * (ch + 1)],
                                x2Td[128 * ch:128 * (ch + 1),
                                     512 * pan_t:512 * (pan_t + 1)])

                for m in range(NCH):
                    @block.tensor
                    def _(t, m=m):
                        trk.mm_group(
                            t, pb[0 + (m % 2)][:],
                            [(wq1_s[:, D * ch + 128 * m:D * ch + 128 * (m + 1)],
                              Span0[:, 512 * ch:512 * (ch + 1)])
                             for ch in range(NCH)])

                    @block.vector
                    def _(v, m=m):
                        dq = QTw[:, 512 * m:512 * (m + 1)]
                        trk.op("dve", v, [pb[0 + (m % 2)][:], gtq[:]], [dq],
                               lambda dq=dq, m=m: v.tensor_tensor(
                                   dq, pb[0 + (m % 2)][:], gtq[:], OP.mult))
                for j in range(4 * pan_t + 4):
                    abuf = (adt0, adt1)[j % 2]
                    ab = 4 + (j % 2)

                    @block.tensor
                    def _(t, j=j, ab=ab):
                        trk.mm_group(
                            t, pb[ab][:],
                            [(KT[:, T * ch + 128 * j:T * ch + 128 * (j + 1)],
                              QTw[:, 512 * ch:512 * (ch + 1)])
                             for ch in range(NCH)])

                    if j < 4 * pan_t:
                        sc_t = g_pow(512 * pan_t - 128 * j)

                        @block.scalar
                        def _(s, ab=ab, abuf=abuf, sc_t=sc_t):
                            trk.op("act", s, [pb[ab][:]], [abuf[:]],
                                   lambda: s.activation(abuf[:], pb[ab][:],
                                                        AF.Copy, scale=sc_t))
                    else:
                        o = j - 4 * pan_t

                        @block.vector
                        def _(v, ab=ab, abuf=abuf, o=o):
                            dm = dcorr[:, 512 * o:512 * (o + 1)]
                            trk.op("dve", v, [pb[ab][:], dm], [abuf[:]],
                                   lambda: v.tensor_tensor(abuf[:], pb[ab][:],
                                                           dm, OP.mult))

                    @block.tensor
                    def _(t, j=j, abuf=abuf, pan_t=pan_t):
                        last = (j == 4 * pan_t + 3)
                        for mc in range(4):
                            trk.mm(t, pb[mc][:],
                                   abuf[:, 128 * mc:128 * (mc + 1)],
                                   V[:, 512 * j:512 * (j + 1)],
                                   start=(j == 0), stop=last)

                for mc in range(4):
                    @block.vector
                    def _(v, hm=hm, mc=mc):
                        ch = 4 * hm + mc
                        dst = y3[:, 512 * ch:512 * (ch + 1)]
                        src = resid[:, 512 * ch:512 * (ch + 1)]
                        trk.op("dve", v, [pb[mc][:], src], [dst],
                               lambda dst=dst, src=src, mc=mc: v.tensor_tensor(
                                   dst, pb[mc][:], src, OP.add))

            emit_ln(trk, block, bufs,
                    lambda ch: y3[:, 512 * ch:512 * (ch + 1)])

            @block.sync
            def _(s, w=w):
                for ch in range(NCH):
                    trk.dma("spd", s, x3gT[128 * ch:128 * (ch + 1),
                                           512 * w:512 * (w + 1)],
                            y3[:, 512 * ch:512 * (ch + 1)])

        @block.sync
        def _(s):
            for i in range(TRK.NDMA):
                s.wait_ge(trk.sems[f"sd{i}"], trk.cnt[f"sd{i}"])
    return nc



def build_C():
    nc = bass.Bass("TRN2", target_bir_lowering=False)
    x3 = nc.dram_tensor("x3gT", [D, 2048], BF16, kind="ExternalInput").ap()
    wgT = nc.dram_tensor("wgT", [D, HID], I32, kind="ExternalInput").ap()
    wuT = nc.dram_tensor("wuT", [D, HID], I32, kind="ExternalInput").ap()
    wdT = nc.dram_tensor("wdT", [HID, D], I32, kind="ExternalInput").ap()
    onescol = nc.dram_tensor("onescol", [128, 1], F32, kind="ExternalInput").ap()
    onesrow = nc.dram_tensor("onesrow", [1, 128], F32, kind="ExternalInput").ap()
    x4T = nc.dram_tensor("x4T", [D, 2048], F32, kind="ExternalOutput").ap()

    NH = HID // 128
    PW = 256
    NPAN = 2048 // PW
    with contextlib.ExitStack() as st_:
        e = st_.enter_context
        wg_s = e(nc.sbuf_tensor("wg_s", [128, NCH * HID], BF16))
        wu_s = e(nc.sbuf_tensor("wu_s", [128, NCH * HID], BF16))
        wd_s = e(nc.sbuf_tensor("wd_s", [128, NH * D], BF16))
        sp0 = e(nc.sbuf_tensor("sp0", [128, NCH * PW], BF16))
        sp1 = e(nc.sbuf_tensor("sp1", [128, NCH * PW], BF16))
        gs = e(nc.sbuf_tensor("gs", [128, PW], BF16))
        us = e(nc.sbuf_tensor("us", [128, PW], BF16))
        hs = e(nc.sbuf_tensor("hs", [128, NH * PW], BF16))
        y4 = e(nc.sbuf_tensor("y4", [128, NCH * PW], F32))
        y4b = e(nc.sbuf_tensor("y4b", [128, NCH * PW], BF16))
        outf = e(nc.sbuf_tensor("outf", [128, NCH * PW], F32))
        bufs = common_bufs(nc, e)
        pb = [e(nc.psum_tensor(f"pb{i}", [128, 512], F32)) for i in range(8)]
        bufs["pb"] = pb
        bufs["ln_banks"] = (6, 7, 4, 5)
        onesb, onesrf = bufs["ones_b"], bufs["onesr_f"]

        trk = TRK(nc, st_)
        block = e(nc.Block())

        @block.gpsimd
        def _(g):
            trk.dma("gpd", g, onesb[:], onescol)
            _loadw(trk, g, wg_s, wgT, NCH, grp=2)
            _loadw(trk, g, wu_s, wuT, NCH, grp=2)
            _loadw(trk, g, wd_s, wdT, NH, grp=4)

        @block.sync
        def _(s):
            trk.dma("spd", s, onesrf[:], onesrow)

        for pan in range(NPAN):
            Sp = (sp0, sp1)[pan % 2]

            @block.sync
            def _(s, pan=pan, Sp=Sp):
                for ch in range(NCH):
                    trk.dma("spd", s, Sp[:, PW * ch:PW * (ch + 1)],
                            x3[128 * ch:128 * (ch + 1), PW * pan:PW * (pan + 1)])

            def scol(ch, Sp=Sp):
                return Sp[:, PW * ch:PW * (ch + 1)]

            for hh in range(NH):
                bg = 0 + (hh % 2)
                bu = 2 + (hh % 2)

                @block.tensor
                def _(t, hh=hh, bg=bg, bu=bu, Sp=Sp):
                    trk.mm_group(t, pb[bg][0:128, 0:PW],
                                 [(wg_s[:, HID * ch + 128 * hh:HID * ch + 128 * (hh + 1)],
                                   scol(ch, Sp)) for ch in range(NCH)])
                    trk.mm_group(t, pb[bu][0:128, 0:PW],
                                 [(wu_s[:, HID * ch + 128 * hh:HID * ch + 128 * (hh + 1)],
                                   scol(ch, Sp)) for ch in range(NCH)])

                @block.scalar
                def _(s, bg=bg):
                    trk.op("act", s, [pb[bg][0:128, 0:PW]], [gs[:]],
                           lambda: s.activation(gs[:], pb[bg][0:128, 0:PW],
                                                AF.Silu, scale=INV127))

                @block.vector
                def _(v, bu=bu, hh=hh):
                    trk.op("dve", v, [pb[bu][0:128, 0:PW]], [us[:]],
                           lambda: v.tensor_scalar(us[:], pb[bu][0:128, 0:PW],
                                                   INV127, None, OP.mult))
                    trk.op("dve", v, [gs[:], us[:]], [hs[:, PW * hh:PW * (hh + 1)]],
                           lambda hh=hh: v.tensor_tensor(
                               hs[:, PW * hh:PW * (hh + 1)], gs[:], us[:],
                               OP.mult))

            for mc in range(NCH):
                bd = 0 + (mc % 2)

                @block.tensor
                def _(t, mc=mc, bd=bd):
                    trk.mm_group(t, pb[bd][0:128, 0:PW],
                                 [(wd_s[:, D * hh + 128 * mc:D * hh + 128 * (mc + 1)],
                                   hs[:, PW * hh:PW * (hh + 1)])
                                  for hh in range(NH)])

                @block.vector
                def _(v, mc=mc, bd=bd, Sp=Sp):
                    dst = y4[:, PW * mc:PW * (mc + 1)]
                    trk.op("dve", v, [pb[bd][0:128, 0:PW], scol(mc, Sp)], [dst],
                           lambda dst=dst, mc=mc, bd=bd, Sp=Sp:
                           v.scalar_tensor_tensor(dst, pb[bd][0:128, 0:PW],
                                                  INV127, scol(mc, Sp),
                                                  OP.mult, OP.add))
                    trk.op("dve", v, [dst], [y4b[:, PW * mc:PW * (mc + 1)]],
                           lambda dst=dst, mc=mc: v.tensor_copy(
                               y4b[:, PW * mc:PW * (mc + 1)], dst))

            mrow, m2row, varrow, rstdrow = (bufs["mrow"], bufs["m2row"],
                                            bufs["varrow"], bufs["rstdrow"])
            mbc, rbc = bufs["mbc"], bufs["rbc"]
            sq = bufs["sqbuf"]

            @block.tensor
            def _(t):
                trk.mm_group(t, pb[6][0:1, 0:PW],
                             [(onesb[:], y4b[:, PW * ch:PW * (ch + 1)])
                              for ch in range(NCH)])

            for ch in range(NCH):
                sq2 = sq[:, 512 * (ch % 2):512 * (ch % 2) + PW]

                @block.scalar
                def _(s, ch=ch, sq2=sq2):
                    trk.op("act", s, [y4b[:, PW * ch:PW * (ch + 1)]], [sq2],
                           lambda: s.activation(
                               sq2, y4b[:, PW * ch:PW * (ch + 1)], AF.Square))

                @block.tensor
                def _(t, ch=ch, sq2=sq2):
                    trk.mm(t, pb[7][0:1, 0:PW], onesb[:], sq2,
                           start=(ch == 0), stop=(ch == NCH - 1))

            @block.vector
            def _(v):
                trk.op("dve", v, [pb[6][0:1, 0:PW]], [mrow[:, 0:PW]],
                       lambda: v.tensor_scalar(mrow[:, 0:PW], pb[6][0:1, 0:PW],
                                               INVD, None, OP.mult))
                trk.op("dve", v, [mrow[:, 0:PW]], [m2row[:, 0:PW]],
                       lambda: v.tensor_tensor(m2row[:, 0:PW], mrow[:, 0:PW],
                                               mrow[:, 0:PW], OP.mult))
                trk.op("dve", v, [m2row[:, 0:PW]], [m2row[:, 0:PW]],
                       lambda: v.tensor_scalar(m2row[:, 0:PW], m2row[:, 0:PW],
                                               LN_EPS, None, OP.subtract))
                trk.op("dve", v, [pb[7][0:1, 0:PW], m2row[:, 0:PW]],
                       [varrow[:, 0:PW]],
                       lambda: v.scalar_tensor_tensor(
                           varrow[:, 0:PW], pb[7][0:1, 0:PW], INVD,
                           m2row[:, 0:PW], OP.mult, OP.subtract))

            @block.scalar
            def _(s):
                trk.op("act", s, [varrow[:, 0:PW]], [rstdrow[:, 0:PW]],
                       lambda: s.activation(rstdrow[:, 0:PW], varrow[:, 0:PW],
                                            AF.Ln))
                trk.op("act", s, [rstdrow[:, 0:PW]], [rstdrow[:, 0:PW]],
                       lambda: s.activation(rstdrow[:, 0:PW], rstdrow[:, 0:PW],
                                            AF.Exp, scale=-0.5))

            @block.tensor
            def _(t):
                trk.mm_group(t, pb[4][0:128, 0:PW], [(onesrf[:], mrow[:, 0:PW])])
                trk.mm_group(t, pb[5][0:128, 0:PW],
                             [(onesrf[:], rstdrow[:, 0:PW])])

            @block.vector
            def _(v):
                trk.op("dve", v, [pb[4][0:128, 0:PW]], [mbc[:, 0:PW]],
                       lambda: v.tensor_copy(mbc[:, 0:PW], pb[4][0:128, 0:PW]))
                trk.op("dve", v, [pb[5][0:128, 0:PW]], [rbc[:, 0:PW]],
                       lambda: v.tensor_copy(rbc[:, 0:PW], pb[5][0:128, 0:PW]))
                for ch in range(NCH):
                    dst = outf[:, PW * ch:PW * (ch + 1)]
                    trk.op("dve", v, [y4[:, PW * ch:PW * (ch + 1)], mbc[:, 0:PW]],
                           [dst],
                           lambda ch=ch, dst=dst: v.tensor_tensor(
                               dst, y4[:, PW * ch:PW * (ch + 1)], mbc[:, 0:PW],
                               OP.subtract))
                    trk.op("dve", v, [dst, rbc[:, 0:PW]], [dst],
                           lambda dst=dst: v.tensor_tensor(
                               dst, dst, rbc[:, 0:PW], OP.mult))

            @block.sync
            def _(s, pan=pan):
                for ch in range(NCH):
                    trk.dma("spd", s, x4T[128 * ch:128 * (ch + 1),
                                          PW * pan:PW * (pan + 1)],
                            outf[:, PW * ch:PW * (ch + 1)])

        @block.sync
        def _(s):
            for i in range(TRK.NDMA):
                s.wait_ge(trk.sems[f"sd{i}"], trk.cnt[f"sd{i}"])
    return nc




def build_F():
    nc = bass.Bass("TRN2", target_bir_lowering=False)
    xT = nc.dram_tensor("xT", [D, T], F32, kind="ExternalInput").ap()
    wq = nc.dram_tensor("wq", [D, DK], F32, kind="ExternalInput").ap()
    wk = nc.dram_tensor("wk", [D, DK], F32, kind="ExternalInput").ap()
    wv = nc.dram_tensor("wv", [D, DK], F32, kind="ExternalInput").ap()
    wo = nc.dram_tensor("wo", [DK, D], F32, kind="ExternalInput").ap()
    wrT = nc.dram_tensor("wrT", [D, NEXP], F32, kind="ExternalInput").ap()
    aT = nc.dram_tensor("aT", [D, NEXP * RNK], F32, kind="ExternalInput").ap()
    bstT = nc.dram_tensor("bstT", [NEXP * RNK, D], F32, kind="ExternalInput").ap()
    emat = nc.dram_tensor("emat", [NEXP, 512], F32, kind="ExternalInput").ap()
    onescol = nc.dram_tensor("onescol", [128, 1], F32, kind="ExternalInput").ap()
    onesrow = nc.dram_tensor("onesrow", [1, 128], F32, kind="ExternalInput").ap()
    hfin = nc.dram_tensor("hfin", [128, 1], F32, kind="ExternalInput").ap()
    wq1 = nc.dram_tensor("wq1", [D, D], F32, kind="ExternalInput").ap()
    wk1 = nc.dram_tensor("wk1", [D, D], F32, kind="ExternalInput").ap()
    wv1h = nc.dram_tensor("wv1h", [D, 512], F32, kind="ExternalInput").ap()
    wgT = nc.dram_tensor("wgT", [D, HID], I32, kind="ExternalInput").ap()
    wuT = nc.dram_tensor("wuT", [D, HID], I32, kind="ExternalInput").ap()
    wdT = nc.dram_tensor("wdT", [HID, D], I32, kind="ExternalInput").ap()
    x2d = nc.dram_tensor("x2d", [D, T], BF16, kind="Internal").ap()
    resGd = nc.dram_tensor("resGd", [D, 2048], BF16, kind="Internal").ap()
    x3d = nc.dram_tensor("x3d", [D, 2048], BF16, kind="Internal").ap()
    x4T = nc.dram_tensor("x4T", [D, 2048], F32, kind="ExternalOutput").ap()

    NWIN = 4
    NH = HID // 128
    PW = 256
    NPAN = 2048 // PW

    with contextlib.ExitStack() as st_:
        e = st_.enter_context
        ARB_ = e(nc.sbuf_tensor("ARB", [128, 88576], BF16))
        ARF_ = e(nc.sbuf_tensor("ARF", [128, 3584], F32))
        ARB = ARB_[:]
        ARF = ARF_[:]
        S = ARB[:, 0:32768]
        QT = ARB[:, 32768:36864]
        KT0 = ARB[:, 36864:40960]
        V0 = ARB[:, 40960:45056]
        wq_s = ARB[:, 45056:46080]
        wk_s = ARB[:, 46080:47104]
        wv_s = ARB[:, 47104:48128]
        wo_s = ARB[:, 48128:49152]
        wrT_s = ARB[:, 49152:49216]
        aT_s = ARB[:, 49216:53312]
        bstT_s = ARB[:, 53312:57408]
        bstT_g = ARB[:, 57408:61504]
        midT = ARB[:, 61504:63552]
        masks = ARB[:, 63552:65600]
        pt0 = ARB[:, 65600:66112]
        pt1 = ARB[:, 66112:66624]
        pt2 = ARB[:, 67136:67648]
        outn = ARB[:, 66624:67136]
        KT1 = ARB[:, 0:32768]
        V1 = ARB[:, 32768:49152]
        wk1_s = ARB[:, 49152:57344]
        wv1_s = ARB[:, 57344:61440]
        wq1_s = ARB[:, 61440:69632]
        QTw = ARB[:, 69632:73728]
        Span = ARB[:, 73728:77824]
        adt0 = ARB[:, 77824:78336]
        adt1 = ARB[:, 78336:78848]
        adt2 = ARB[:, 79872:80384]
        gtq = ARB[:, 78848:79360]
        gts = ARB[:, 79360:79872]
        resid = ARB[:, 80384:84480]
        y3 = ARB[:, 84480:88576]
        wg_s = ARB[:, 0:24576]
        wu_s = ARB[:, 24576:49152]
        wd_s = ARB[:, 49152:73728]
        hs = ARB[:, 73728:79872]
        sp0 = ARB[:, 79872:81920]
        sp1 = ARB[:, 81920:83968]
        y4b = ARB[:, 83968:86016]
        gs = ARB[:, 86016:86272]
        us = ARB[:, 86272:86528]
        es_ = ARB[:, 86528:86784]
        es2_ = ARB[:, 86784:87040]
        gu_ = ARB[:, 87040:87296]
        mbc = ARF[:, 0:512]
        rbc = ARF[:, 512:1024]
        recbc = ARF[:, 1024:1536]
        den_ = ARF[:, 1024:1280]
        rc_ = ARF[:, 1280:1536]
        iof = ARF[:, 1536:3584]
        dcorr = ARF[:, 1536:3584]
        y4 = ARF[:, 1536:3584]
        sqbuf = e(nc.sbuf_tensor("sqbuf", [128, 2 * 512], BF16))
        tmp0 = e(nc.sbuf_tensor("tmp0", [128, 512], BF16))
        tmp1 = e(nc.sbuf_tensor("tmp1", [128, 512], BF16))
        mrow = e(nc.sbuf_tensor("mrow", [1, 512], F32))
        m2row = e(nc.sbuf_tensor("m2row", [1, 512], F32))
        varrow = e(nc.sbuf_tensor("varrow", [1, 512], F32))
        rstdrow = e(nc.sbuf_tensor("rstdrow", [1, 512], F32))
        ones_b = e(nc.sbuf_tensor("ones_b", [128, 1], BF16))
        onesr_f = e(nc.sbuf_tensor("onesr_f", [1, 128], F32))
        one_f = e(nc.sbuf_tensor("one_f", [1, 1], F32))
        l8 = e(nc.sbuf_tensor("l8", [1, 64], F32))
        g8 = e(nc.sbuf_tensor("g8", [NEXP, 1], F32))
        gcol = e(nc.sbuf_tensor("gcol", [128, 4], F32))
        emat_s = e(nc.sbuf_tensor("emat_s", [NEXP, 512], F32))
        hf = e(nc.sbuf_tensor("hf", [128, 1], F32))
        pb = [e(nc.psum_tensor(f"pb{i}", [128, 512], F32)) for i in range(8)]

        bufs = dict(pb=pb, sqbuf=sqbuf, mrow=mrow, m2row=m2row, varrow=varrow,
                    rstdrow=rstdrow, mbc=mbc, rbc=rbc, tmp0=tmp0, tmp1=tmp1,
                    ones_b=ones_b, onesr_f=onesr_f, ln_banks=(7, 4, 5, 6))
        trk = TRK(nc, st_)
        block = _FakeBlock()

        def emit_ln_f(s_ap, width=512):
            W = width
            b_sy, b_sy2, b_m, b_r = bufs["ln_banks"]

            @block.tensor
            def _(t):
                trk.mm_group(t, pb[b_sy][0:1, 0:W],
                             [(ones_b[:], s_ap(ch)) for ch in range(NCH)])

            for ch in range(NCH):
                sq = sqbuf[:, 512 * (ch % 2):512 * (ch % 2) + W]

                @block.scalar
                def _(s, ch=ch, sq=sq):
                    trk.op("act", s, [s_ap(ch)], [sq],
                           lambda: s.activation(sq, s_ap(ch), AF.Square))

                @block.tensor
                def _(t, ch=ch, sq=sq):
                    trk.mm(t, pb[b_sy2][0:1, 0:W], ones_b[:], sq,
                           start=(ch == 0), stop=(ch == NCH - 1))

            @block.vector
            def _(v):
                trk.op("dve", v, [pb[b_sy][0:1, 0:W]], [mrow[:, 0:W]],
                       lambda: v.tensor_scalar(mrow[:, 0:W], pb[b_sy][0:1, 0:W],
                                               INVD, None, OP.mult))
                trk.op("dve", v, [mrow[:, 0:W]], [m2row[:, 0:W]],
                       lambda: v.tensor_tensor(m2row[:, 0:W], mrow[:, 0:W],
                                               mrow[:, 0:W], OP.mult))
                trk.op("dve", v, [m2row[:, 0:W]], [m2row[:, 0:W]],
                       lambda: v.tensor_scalar(m2row[:, 0:W], m2row[:, 0:W],
                                               LN_EPS, None, OP.subtract))
                trk.op("dve", v, [pb[b_sy2][0:1, 0:W], m2row[:, 0:W]],
                       [varrow[:, 0:W]],
                       lambda: v.scalar_tensor_tensor(
                           varrow[:, 0:W], pb[b_sy2][0:1, 0:W], INVD,
                           m2row[:, 0:W], OP.mult, OP.subtract))

            @block.scalar
            def _(s):
                trk.op("act", s, [varrow[:, 0:W]], [rstdrow[:, 0:W]],
                       lambda: s.activation(rstdrow[:, 0:W], varrow[:, 0:W],
                                            AF.Ln))
                trk.op("act", s, [rstdrow[:, 0:W]], [rstdrow[:, 0:W]],
                       lambda: s.activation(rstdrow[:, 0:W], rstdrow[:, 0:W],
                                            AF.Exp, scale=-0.5))

            @block.tensor
            def _(t):
                trk.mm_group(t, pb[b_m][0:128, 0:W], [(onesr_f[:], mrow[:, 0:W])])
                trk.mm_group(t, pb[b_r][0:128, 0:W],
                             [(onesr_f[:], rstdrow[:, 0:W])])

            @block.vector
            def _(v):
                trk.op("dve", v, [pb[b_m][0:128, 0:W]], [mbc[:, 0:W]],
                       lambda: v.tensor_copy(mbc[:, 0:W], pb[b_m][0:128, 0:W]))
                trk.op("dve", v, [pb[b_r][0:128, 0:W]], [rbc[:, 0:W]],
                       lambda: v.tensor_copy(rbc[:, 0:W], pb[b_r][0:128, 0:W]))
                for ch in range(NCH):
                    tmp = (tmp0, tmp1)[ch % 2]
                    trk.op("dve", v, [s_ap(ch), mbc[:, 0:W]], [tmp[:, 0:W]],
                           lambda ch=ch, tmp=tmp: v.tensor_tensor(
                               tmp[:, 0:W], s_ap(ch), mbc[:, 0:W], OP.subtract))
                    trk.op("dve", v, [tmp[:, 0:W], rbc[:, 0:W]], [s_ap(ch)],
                           lambda ch=ch, tmp=tmp: v.tensor_tensor(
                               s_ap(ch), tmp[:, 0:W], rbc[:, 0:W], OP.mult))

        @block.gpsimd
        def _(g):
            for ch in range(NCH):
                trk.dma("gpd", g, S[:, T * ch:T * (ch + 1)],
                        xT[128 * ch:128 * (ch + 1), :])
            _loadw(trk, g, wq_s, wq, NCH)
            _loadw(trk, g, wk_s, wk, NCH)
            _loadw(trk, g, wv_s, wv, NCH)
            _loadw(trk, g, wo_s, wo, 1)
            _loadw(trk, g, wrT_s, wrT, NCH)
            _loadw(trk, g, aT_s, aT, NCH)
            _loadw(trk, g, bstT_s, bstT, 4)
            trk.dma("gpd", g, ones_b[:], onescol)
            trk.dma("gpd", g, hf[:], hfin)
            for o in range(4):
                trk.op("pool", g, [], [iof[:, 512 * o:512 * (o + 1)]],
                       lambda o=o: g.iota(iof[:, 512 * o:512 * (o + 1)],
                                          pattern=[[1, 512]], base=-128 * o,
                                          channel_multiplier=-1,
                                          allow_small_or_imprecise_dtypes=True))

        @block.sync
        def _(s):
            trk.dma("spd", s, emat_s[:], emat)
            trk.dma("spd", s, onesr_f[:], onesrow)
            trk.dma("spd", s, one_f[:], onesrow[:, 0:1])

        @block.vector
        def _(v):
            trk.op("dve", v, [iof[:]], [masks[:]],
                   lambda: v.tensor_scalar(masks[:], iof[:], 0.0, None,
                                           OP.is_ge))

        for w in range(NW):
            @block.tensor
            def _(t, w=w):
                for i, wsb in enumerate((wq_s, wk_s)):
                    trk.mm_group(t, pb[i][:],
                                 [(wsb[:, DK * ch:DK * (ch + 1)],
                                   S[:, T * ch + 512 * w:T * ch + 512 * (w + 1)])
                                  for ch in range(NCH)])

            @block.scalar
            def _(s, w=w):
                for i, dst in enumerate((QT, KT0)):
                    trk.op("act", s, [pb[i][:]], [dst[:, 512 * w:512 * (w + 1)]],
                           lambda i=i, dst=dst, w=w: s.copy(
                               dst[:, 512 * w:512 * (w + 1)], pb[i][:]))
        for j in range(T // 128):
            @block.tensor
            def _(t, j=j):
                trk.mm_group(t, pb[2 + (j % 2)][0:128, 0:128],
                             [(S[:, T * ch + 128 * j:T * ch + 128 * (j + 1)],
                               wv_s[:, DK * ch:DK * (ch + 1)]) for ch in range(NCH)])

            @block.vector
            def _(v, j=j):
                trk.op("dve", v, [pb[2 + (j % 2)][0:128, 0:128]],
                       [V0[:, 128 * j:128 * (j + 1)]],
                       lambda j=j: v.tensor_copy(V0[:, 128 * j:128 * (j + 1)],
                                                 pb[2 + (j % 2)][0:128, 0:128]))

        def s_chunk(w):
            return lambda ch: S[:, T * ch + 512 * w:T * ch + 512 * (w + 1)]

        for w in range(NW):
            npj = 4 * w + 4
            for j in range(npj):
                pt = (pt0, pt1, pt2)[j % 3]
                bank = pb[(0, 1, 4)[j % 3]]

                @block.tensor
                def _(t, w=w, j=j, bank=bank):
                    trk.mm_group(t, bank[:],
                                 [(KT0[:, 128 * j:128 * (j + 1)],
                                   QT[:, 512 * w:512 * (w + 1)])])

                @block.scalar
                def _(s, pt=pt, bank=bank):
                    trk.op("act", s, [bank[:]], [pt],
                           lambda: s.activation(pt, bank[:], AF.Exp,
                                                scale=SCALE_QK))
                if j >= 4 * w:
                    o = j - 4 * w

                    @block.vector
                    def _(v, o=o, pt=pt):
                        trk.op("dve", v, [pt, masks[:, 512 * o:512 * (o + 1)]],
                               [pt],
                               lambda: v.tensor_tensor(
                                   pt, pt, masks[:, 512 * o:512 * (o + 1)],
                                   OP.mult))

                @block.tensor
                def _(t, j=j, pt=pt, npj=npj):
                    trk.mm(t, pb[2][:], V0[:, 128 * j:128 * (j + 1)], pt,
                           start=(j == 0), stop=(j == npj - 1))
                    trk.mm(t, pb[3][0:1, 0:512], ones_b[:], pt,
                           start=(j == 0), stop=(j == npj - 1))

            @block.vector
            def _(v):
                trk.op("dve", v, [pb[3][0:1, 0:512]], [mrow[:]],
                       lambda: v.tensor_copy(mrow[:], pb[3][0:1, 0:512]))
                trk.op("dve", v, [mrow[:]], [m2row[:]],
                       lambda: v.reciprocal(m2row[:], mrow[:]))

            @block.tensor
            def _(t):
                trk.mm_group(t, pb[4][:], [(onesr_f[:], m2row[:])])

            @block.vector
            def _(v):
                trk.op("dve", v, [pb[4][:]], [recbc[:]],
                       lambda: v.tensor_copy(recbc[:], pb[4][:]))
                trk.op("dve", v, [pb[2][:], recbc[:]], [outn],
                       lambda: v.tensor_tensor(outn, pb[2][:], recbc[:],
                                               OP.mult))

            for m in range(NCH):
                @block.tensor
                def _(t, m=m):
                    trk.mm_group(t, pb[5 + (m % 2)][:],
                                 [(wo_s[:, 128 * m:128 * (m + 1)], outn)])

                @block.vector
                def _(v, w=w, m=m):
                    sap = s_chunk(w)(m)
                    trk.op("dve", v, [pb[5 + (m % 2)][:], sap], [sap],
                           lambda sap=sap, m=m: v.tensor_tensor(
                               sap, pb[5 + (m % 2)][:], sap, OP.add))
            emit_ln_f(s_chunk(w))

        @block.tensor
        def _(t):
            trk.mm_group(t, pb[0][0:1, 0:NEXP],
                         [(S[:, T * ch:T * ch + 1],
                           wrT_s[:, NEXP * ch:NEXP * (ch + 1)])
                          for ch in range(NCH)])

        @block.vector
        def _(v):
            L = l8
            trk.op("dve", v, [pb[0][0:1, 0:NEXP]], [L[:, 0:8]],
                   lambda: v.tensor_copy(L[:, 0:8], pb[0][0:1, 0:NEXP]))
            trk.op("dve", v, [L[:, 0:8]], [L[:, 8:9]],
                   lambda: v.reduce_max(L[:, 8:9], L[:, 0:8], axis=AX.X))
            trk.op("dve", v, [L[:, 8:9]], [L[:, 9:10]],
                   lambda: v.tensor_scalar(L[:, 9:10], L[:, 8:9], -1.0, None,
                                           OP.mult))
            trk.op("dve", v, [L[:, 0:8], L[:, 8:9]], [L[:, 16:24]],
                   lambda: v.tensor_scalar(L[:, 16:24], L[:, 0:8], L[:, 8:9],
                                           None, OP.is_ge))
            trk.op("dve", v, [L[:, 16:24], L[:, 0:8]], [L[:, 24:32]],
                   lambda: v.scalar_tensor_tensor(L[:, 24:32], L[:, 16:24],
                                                  -1e30, L[:, 0:8], OP.mult,
                                                  OP.add))
            trk.op("dve", v, [L[:, 24:32]], [L[:, 10:11]],
                   lambda: v.reduce_max(L[:, 10:11], L[:, 24:32], axis=AX.X))
            trk.op("dve", v, [L[:, 0:8], L[:, 10:11]], [L[:, 16:24]],
                   lambda: v.tensor_scalar(L[:, 16:24], L[:, 0:8], L[:, 10:11],
                                           None, OP.is_ge))

        @block.scalar
        def _(s):
            trk.op("act", s, [l8[:, 0:8], l8[:, 9:10]], [l8[:, 32:40]],
                   lambda: s.activation(l8[:, 32:40], l8[:, 0:8], AF.Exp,
                                        bias=l8[:, 9:10]))

        @block.vector
        def _(v):
            L = l8
            trk.op("dve", v, [L[:, 32:40], L[:, 16:24]], [L[:, 40:48]],
                   lambda: v.tensor_tensor(L[:, 40:48], L[:, 32:40],
                                           L[:, 16:24], OP.mult))
            trk.op("dve", v, [L[:, 40:48]], [L[:, 11:12]],
                   lambda: v.reduce_sum(L[:, 11:12], L[:, 40:48], axis=AX.X))
            trk.op("dve", v, [L[:, 11:12]], [L[:, 12:13]],
                   lambda: v.reciprocal(L[:, 12:13], L[:, 11:12]))
            trk.op("dve", v, [L[:, 40:48], L[:, 12:13]], [L[:, 48:56]],
                   lambda: v.tensor_scalar(L[:, 48:56], L[:, 40:48],
                                           L[:, 12:13], None, OP.mult))

        @block.tensor
        def _(t):
            trk.mm_group(t, pb[1][0:NEXP, 0:1], [(l8[:, 48:56], one_f[:])])

        @block.vector
        def _(v):
            trk.op("dve", v, [pb[1][0:NEXP, 0:1]], [g8[:]],
                   lambda: v.tensor_copy(g8[:], pb[1][0:NEXP, 0:1]))

        for i in range(4):
            @block.tensor
            def _(t, i=i):
                trk.mm_group(t, pb[2][0:128, 0:1],
                             [(emat_s[:, 128 * i:128 * (i + 1)], g8[:])])

            @block.vector
            def _(v, i=i):
                trk.op("dve", v, [pb[2][0:128, 0:1]], [gcol[:, i:i + 1]],
                       lambda i=i: v.tensor_copy(gcol[:, i:i + 1],
                                                 pb[2][0:128, 0:1]))

        @block.vector
        def _(v):
            for i in range(4):
                trk.op("dve", v, [bstT_s[:, D * i:D * (i + 1)],
                                  gcol[:, i:i + 1]],
                       [bstT_g[:, D * i:D * (i + 1)]],
                       lambda i=i: v.tensor_scalar(
                           bstT_g[:, D * i:D * (i + 1)],
                           bstT_s[:, D * i:D * (i + 1)],
                           gcol[:, i:i + 1], None, OP.mult))

        for w in range(NW):
            for m in range(4):
                @block.tensor
                def _(t, w=w, m=m):
                    trk.mm_group(
                        t, pb[m % 2][:],
                        [(aT_s[:, 512 * ch + 128 * m:512 * ch + 128 * (m + 1)],
                          S[:, T * ch + 512 * w:T * ch + 512 * (w + 1)])
                         for ch in range(NCH)])

                @block.scalar
                def _(s, m=m):
                    trk.op("act", s, [pb[m % 2][:]],
                           [midT[:, 512 * m:512 * (m + 1)]],
                           lambda m=m: s.copy(midT[:, 512 * m:512 * (m + 1)],
                                              pb[m % 2][:]))
            for m in range(NCH):
                @block.tensor
                def _(t, m=m):
                    trk.mm_group(
                        t, pb[5 + (m % 2)][:],
                        [(bstT_g[:, D * rc + 128 * m:D * rc + 128 * (m + 1)],
                          midT[:, 512 * rc:512 * (rc + 1)]) for rc in range(4)])

                @block.vector
                def _(v, w=w, m=m):
                    sap = s_chunk(w)(m)
                    trk.op("dve", v, [pb[5 + (m % 2)][:], sap], [sap],
                           lambda sap=sap, m=m: v.scalar_tensor_tensor(
                               sap, sap, 2.0, pb[5 + (m % 2)][:], OP.mult,
                               OP.add))
            emit_ln_f(s_chunk(w))

        @block.gpsimd
        def _(g):
            _loadw(trk, g, wk1_s, wk1, NCH)
            _loadw(trk, g, wv1_s, wv1h, NCH)
            _loadw(trk, g, wq1_s, wq1, NCH)

        @block.sync
        def _(s):
            for ch in range(NCH):
                trk.dma("spd", s, x2d[128 * ch:128 * (ch + 1), :],
                        S[:, T * ch:T * (ch + 1)])

        for w in range(NWIN):
            for ch in range(NCH):
                s0 = S[:, T * ch + w:T * ch + 2048:4]
                s1 = S[:, T * ch + 2048 + w:T * ch + 4096:4]

                @block.vector
                def _(v, s0=s0, s1=s1):
                    trk.op("dve", v, [s1, s0], [tmp0[:]],
                           lambda: v.tensor_tensor(tmp0[:], s1, s0,
                                                   OP.subtract))
                    trk.op("dve", v, [tmp0[:], hf[:], s0], [tmp1[:]],
                           lambda: v.scalar_tensor_tensor(
                               tmp1[:], tmp0[:], hf[:, 0:1], s0, OP.mult,
                               OP.add))

                @block.sync
                def _(s, w=w, ch=ch):
                    trk.dma("spd", s, resGd[128 * ch:128 * (ch + 1),
                                            512 * w:512 * (w + 1)], tmp1[:])

        @block.gpsimd
        def _(g):
            trk.op("pool", g, [], [iof[:, 0:512]],
                   lambda: g.iota(iof[:, 0:512], pattern=[[1, 512]], base=0,
                                  channel_multiplier=0,
                                  allow_small_or_imprecise_dtypes=True))

        @block.scalar
        def _(s):
            trk.op("act", s, [iof[:, 0:512]], [gtq],
                   lambda: s.activation(gtq, iof[:, 0:512], AF.Exp, scale=LNG))

        @block.gpsimd
        def _(g):
            trk.op("pool", g, [], [iof[:, 0:512]],
                   lambda: g.iota(iof[:, 0:512], pattern=[[0, 4], [1, 128]],
                                  base=0, channel_multiplier=0,
                                  allow_small_or_imprecise_dtypes=True))

        @block.scalar
        def _(s):
            trk.op("act", s, [iof[:, 0:512]], [gts],
                   lambda: s.activation(gts, iof[:, 0:512], AF.Exp, scale=-LNG))

        @block.gpsimd
        def _(g):
            for o in range(4):
                trk.op("pool", g, [], [dcorr[:, 512 * o:512 * (o + 1)]],
                       lambda o=o: g.iota(dcorr[:, 512 * o:512 * (o + 1)],
                                          pattern=[[1, 512]], base=-128 * o,
                                          channel_multiplier=-1,
                                          allow_small_or_imprecise_dtypes=True))

        @block.vector
        def _(v):
            for o in range(4):
                dst = dcorr[:, 512 * o:512 * (o + 1)]
                trk.op("dve", v, [dst], [dst],
                       lambda dst=dst: v.tensor_scalar(dst, dst, 0.0, None,
                                                       OP.is_ge))
                trk.op("dve", v, [dst], [dst],
                       lambda dst=dst, o=o: v.tensor_scalar(
                           dst, dst, g_pow(-128 * o), None, OP.mult))

        for pan in range(NW):
            Spp = Span if pan % 2 == 0 else y3[:, 0:4096]

            @block.sync
            def _(s, pan=pan, Spp=Spp):
                for ch in range(NCH):
                    trk.dma("spd", s, Spp[:, 512 * ch:512 * (ch + 1)],
                            x2d[128 * ch:128 * (ch + 1),
                                512 * pan:512 * (pan + 1)])

            for m in range(NCH):
                @block.tensor
                def _(t, m=m, Spp=Spp):
                    trk.mm_group(t, pb[0 + (m % 2)][:],
                                 [(wk1_s[:, D * ch + 128 * m:D * ch + 128 * (m + 1)],
                                   Spp[:, 512 * ch:512 * (ch + 1)])
                                  for ch in range(NCH)])

                @block.vector
                def _(v, pan=pan, m=m):
                    dk = KT1[:, T * m + 512 * pan:T * m + 512 * (pan + 1)]
                    trk.op("dve", v, [pb[0 + (m % 2)][:], gts], [dk],
                           lambda dk=dk, m=m: v.tensor_tensor(
                               dk, pb[0 + (m % 2)][:], gts, OP.mult))

            for jj in range(4):
                j = 4 * pan + jj

                @block.tensor
                def _(t, jj=jj, Spp=Spp):
                    trk.mm_group(
                        t, pb[2 + (jj % 2)][:],
                        [(Spp[:, 512 * ch + 128 * jj:512 * ch + 128 * (jj + 1)],
                          wv1_s[:, 512 * ch:512 * (ch + 1)])
                         for ch in range(NCH)])

                @block.scalar
                def _(s, jj=jj, j=j):
                    trk.op("act", s, [pb[2 + (jj % 2)][:]],
                           [V1[:, 512 * j:512 * (j + 1)]],
                           lambda jj=jj, j=j: s.copy(
                               V1[:, 512 * j:512 * (j + 1)],
                               pb[2 + (jj % 2)][:]))

        for w in range(NWIN):
            @block.sync
            def _(s, w=w):
                for ch in range(NCH):
                    trk.dma("spd", s, resid[:, 512 * ch:512 * (ch + 1)],
                            resGd[128 * ch:128 * (ch + 1),
                                  512 * w:512 * (w + 1)])

            for hm in range(2):
                pan_t = 2 * w + hm

                @block.sync
                def _(s, pan_t=pan_t):
                    for ch in range(NCH):
                        trk.dma("spd", s, Span[:, 512 * ch:512 * (ch + 1)],
                                x2d[128 * ch:128 * (ch + 1),
                                    512 * pan_t:512 * (pan_t + 1)])

                for m in range(NCH):
                    @block.tensor
                    def _(t, m=m):
                        trk.mm_group(
                            t, pb[0 + (m % 2)][:],
                            [(wq1_s[:, D * ch + 128 * m:D * ch + 128 * (m + 1)],
                              Span[:, 512 * ch:512 * (ch + 1)])
                             for ch in range(NCH)])

                    @block.vector
                    def _(v, m=m):
                        dq = QTw[:, 512 * m:512 * (m + 1)]
                        trk.op("dve", v, [pb[0 + (m % 2)][:], gtq], [dq],
                               lambda dq=dq, m=m: v.tensor_tensor(
                                   dq, pb[0 + (m % 2)][:], gtq, OP.mult))

                for j in range(4 * pan_t + 4):
                    abuf = (adt0, adt1, adt2)[j % 3]
                    ab = (4, 5, 6)[j % 3]

                    @block.tensor
                    def _(t, j=j, ab=ab):
                        trk.mm_group(
                            t, pb[ab][:],
                            [(KT1[:, T * ch + 128 * j:T * ch + 128 * (j + 1)],
                              QTw[:, 512 * ch:512 * (ch + 1)])
                             for ch in range(NCH)])

                    if j < 4 * pan_t:
                        sc_t = g_pow(512 * pan_t - 128 * j)

                        @block.scalar
                        def _(s, ab=ab, abuf=abuf, sc_t=sc_t):
                            trk.op("act", s, [pb[ab][:]], [abuf],
                                   lambda: s.activation(abuf, pb[ab][:],
                                                        AF.Copy, scale=sc_t))
                    else:
                        o = j - 4 * pan_t

                        @block.vector
                        def _(v, ab=ab, abuf=abuf, o=o):
                            dm = dcorr[:, 512 * o:512 * (o + 1)]
                            trk.op("dve", v, [pb[ab][:], dm], [abuf],
                                   lambda: v.tensor_tensor(abuf, pb[ab][:],
                                                           dm, OP.mult))

                    @block.tensor
                    def _(t, j=j, abuf=abuf, pan_t=pan_t):
                        last = (j == 4 * pan_t + 3)
                        for mc in range(4):
                            trk.mm(t, pb[mc][:],
                                   abuf[:, 128 * mc:128 * (mc + 1)],
                                   V1[:, 512 * j:512 * (j + 1)],
                                   start=(j == 0), stop=last)

                for mc in range(4):
                    @block.vector
                    def _(v, hm=hm, mc=mc):
                        ch = 4 * hm + mc
                        dst = y3[:, 512 * ch:512 * (ch + 1)]
                        src = resid[:, 512 * ch:512 * (ch + 1)]
                        trk.op("dve", v, [pb[mc][:], src], [dst],
                               lambda dst=dst, src=src, mc=mc: v.tensor_tensor(
                                   dst, pb[mc][:], src, OP.add))

            emit_ln_f(lambda ch: y3[:, 512 * ch:512 * (ch + 1)])

            @block.sync
            def _(s, w=w):
                for ch in range(NCH):
                    trk.dma("spd", s, x3d[128 * ch:128 * (ch + 1),
                                          512 * w:512 * (w + 1)],
                            y3[:, 512 * ch:512 * (ch + 1)])

        @block.gpsimd
        def _(g):
            _loadw(trk, g, wg_s, wgT, NCH, grp=2)
            _loadw(trk, g, wu_s, wuT, NCH, grp=2)
            _loadw(trk, g, wd_s, wdT, NH, grp=4)

        for pan in range(NPAN):
            Sp = (sp0, sp1)[pan % 2]

            @block.sync
            def _(s, pan=pan, Sp=Sp):
                for ch in range(NCH):
                    trk.dma("spd", s, Sp[:, PW * ch:PW * (ch + 1)],
                            x3d[128 * ch:128 * (ch + 1),
                                PW * pan:PW * (pan + 1)])

            def scol(ch, Sp=Sp):
                return Sp[:, PW * ch:PW * (ch + 1)]

            for hh in range(NH):
                bg = 4 + (hh % 2)
                bu = 6 + (hh % 2)

                @block.tensor
                def _(t, hh=hh, bg=bg, bu=bu, Sp=Sp):
                    trk.mm_group(t, pb[bg][0:128, 0:PW],
                                 [(wg_s[:, HID * ch + 128 * hh:HID * ch + 128 * (hh + 1)],
                                   scol(ch, Sp)) for ch in range(NCH)])
                    trk.mm_group(t, pb[bu][0:128, 0:PW],
                                 [(wu_s[:, HID * ch + 128 * hh:HID * ch + 128 * (hh + 1)],
                                   scol(ch, Sp)) for ch in range(NCH)])

                @block.scalar
                def _(s, bg=bg):
                    trk.op("act", s, [pb[bg][0:128, 0:PW]], [gs],
                           lambda: s.activation(gs, pb[bg][0:128, 0:PW],
                                                AF.Silu, scale=INV127))

                @block.vector
                def _(v, bu=bu, hh=hh):
                    trk.op("dve", v, [pb[bu][0:128, 0:PW]], [us],
                           lambda: v.tensor_scalar(us, pb[bu][0:128, 0:PW],
                                                   INV127, None, OP.mult))
                    trk.op("dve", v, [gs, us], [hs[:, PW * hh:PW * (hh + 1)]],
                           lambda hh=hh: v.tensor_tensor(
                               hs[:, PW * hh:PW * (hh + 1)], gs, us, OP.mult))

            for mc in range(NCH):
                bd = 0 + (mc % 2)

                @block.tensor
                def _(t, mc=mc, bd=bd):
                    trk.mm_group(t, pb[bd][0:128, 0:PW],
                                 [(wd_s[:, D * hh + 128 * mc:D * hh + 128 * (mc + 1)],
                                   hs[:, PW * hh:PW * (hh + 1)])
                                  for hh in range(NH)])

                @block.vector
                def _(v, mc=mc, bd=bd, Sp=Sp):
                    dst = y4[:, PW * mc:PW * (mc + 1)]
                    trk.op("dve", v, [pb[bd][0:128, 0:PW], scol(mc, Sp)], [dst],
                           lambda dst=dst, mc=mc, bd=bd, Sp=Sp:
                           v.scalar_tensor_tensor(dst, pb[bd][0:128, 0:PW],
                                                  INV127, scol(mc, Sp),
                                                  OP.mult, OP.add))
                    trk.op("dve", v, [dst], [y4b[:, PW * mc:PW * (mc + 1)]],
                           lambda dst=dst, mc=mc: v.tensor_copy(
                               y4b[:, PW * mc:PW * (mc + 1)], dst))

            mrow, m2row, varrow, rstdrow = (bufs["mrow"], bufs["m2row"],
                                            bufs["varrow"], bufs["rstdrow"])
            mbc, rbc = bufs["mbc"], bufs["rbc"]
            sq = bufs["sqbuf"]

            @block.tensor
            def _(t):
                trk.mm_group(t, pb[6][0:1, 0:PW],
                             [(ones_b[:], y4b[:, PW * ch:PW * (ch + 1)])
                              for ch in range(NCH)])

            for ch in range(NCH):
                sq2 = sq[:, 512 * (ch % 2):512 * (ch % 2) + PW]

                @block.scalar
                def _(s, ch=ch, sq2=sq2):
                    trk.op("act", s, [y4b[:, PW * ch:PW * (ch + 1)]], [sq2],
                           lambda: s.activation(
                               sq2, y4b[:, PW * ch:PW * (ch + 1)], AF.Square))

                @block.tensor
                def _(t, ch=ch, sq2=sq2):
                    trk.mm(t, pb[7][0:1, 0:PW], ones_b[:], sq2,
                           start=(ch == 0), stop=(ch == NCH - 1))

            @block.vector
            def _(v):
                trk.op("dve", v, [pb[6][0:1, 0:PW]], [mrow[:, 0:PW]],
                       lambda: v.tensor_scalar(mrow[:, 0:PW], pb[6][0:1, 0:PW],
                                               INVD, None, OP.mult))
                trk.op("dve", v, [mrow[:, 0:PW]], [m2row[:, 0:PW]],
                       lambda: v.tensor_tensor(m2row[:, 0:PW], mrow[:, 0:PW],
                                               mrow[:, 0:PW], OP.mult))
                trk.op("dve", v, [m2row[:, 0:PW]], [m2row[:, 0:PW]],
                       lambda: v.tensor_scalar(m2row[:, 0:PW], m2row[:, 0:PW],
                                               LN_EPS, None, OP.subtract))
                trk.op("dve", v, [pb[7][0:1, 0:PW], m2row[:, 0:PW]],
                       [varrow[:, 0:PW]],
                       lambda: v.scalar_tensor_tensor(
                           varrow[:, 0:PW], pb[7][0:1, 0:PW], INVD,
                           m2row[:, 0:PW], OP.mult, OP.subtract))

            @block.scalar
            def _(s):
                trk.op("act", s, [varrow[:, 0:PW]], [rstdrow[:, 0:PW]],
                       lambda: s.activation(rstdrow[:, 0:PW], varrow[:, 0:PW],
                                            AF.Ln))
                trk.op("act", s, [rstdrow[:, 0:PW]], [rstdrow[:, 0:PW]],
                       lambda: s.activation(rstdrow[:, 0:PW], rstdrow[:, 0:PW],
                                            AF.Exp, scale=-0.5))

            @block.tensor
            def _(t):
                trk.mm_group(t, pb[4][0:128, 0:PW], [(onesr_f[:], mrow[:, 0:PW])])
                trk.mm_group(t, pb[5][0:128, 0:PW],
                             [(onesr_f[:], rstdrow[:, 0:PW])])

            @block.vector
            def _(v):
                trk.op("dve", v, [pb[4][0:128, 0:PW]], [mbc[:, 0:PW]],
                       lambda: v.tensor_copy(mbc[:, 0:PW], pb[4][0:128, 0:PW]))
                trk.op("dve", v, [pb[5][0:128, 0:PW]], [rbc[:, 0:PW]],
                       lambda: v.tensor_copy(rbc[:, 0:PW], pb[5][0:128, 0:PW]))
                for ch in range(NCH):
                    dst = y4[:, PW * ch:PW * (ch + 1)]
                    trk.op("dve", v, [dst, mbc[:, 0:PW]], [dst],
                           lambda ch=ch, dst=dst: v.tensor_tensor(
                               dst, dst, mbc[:, 0:PW], OP.subtract))
                    trk.op("dve", v, [dst, rbc[:, 0:PW]], [dst],
                           lambda dst=dst: v.tensor_tensor(
                               dst, dst, rbc[:, 0:PW], OP.mult))

            @block.sync
            def _(s, pan=pan):
                for ch in range(NCH):
                    trk.dma("spd", s, x4T[128 * ch:128 * (ch + 1),
                                          PW * pan:PW * (pan + 1)],
                            y4[:, PW * ch:PW * (ch + 1)])

        @block.sync
        def _(s):
            for i in range(TRK.NDMA):
                s.wait_ge(trk.sems[f"sd{i}"], trk.cnt[f"sd{i}"])
            for i in range(TRK.NDMA):
                s.wait_ge(trk.sems[f"gd{i}"], trk.cnt[f"gd{i}"])

        with nc.Block() as rb:
            @rb.tensor
            def _(t):
                block.recs["pe"].replay(t)

            @rb.scalar
            def _(s):
                block.recs["act"].replay(s)

            @rb.vector
            def _(v):
                block.recs["dve"].replay(v)

            @rb.gpsimd
            def _(g):
                block.recs["pool"].replay(g)

            @rb.sync
            def _(s):
                block.recs["spd"].replay(s)
    return nc




_NC_CACHE = {}


def _get(name, fn):
    if name not in _NC_CACHE:
        _NC_CACHE[name] = fn()
    return _NC_CACHE[name]


def kernel(L0_attn_x, L0_attn_Wq, L0_attn_Wk, L0_attn_Wv, L0_attn_Wo,
           L0_ffn_Wr, L0_ffn_A, L0_ffn_Bw, L1_attn_att_mask, L1_attn_S_n,
           L1_attn_W_Q, L1_attn_W_K, L1_attn_W_V, L1_ffn_w_gate, L1_ffn_w_up,
           L1_ffn_w_down):
    _ = (L1_attn_att_mask, L1_attn_S_n)
    C = np.ascontiguousarray
    onescol = np.ones((128, 1), np.float32)
    onesrow = np.ones((1, 128), np.float32)
    wrT = C(np.asarray(L0_ffn_Wr).T)
    aT = C(np.asarray(L0_ffn_A).reshape(NEXP * RNK, D).T)
    bstT = C(np.transpose(np.asarray(L0_ffn_Bw), (0, 2, 1)).reshape(NEXP * RNK, D))
    emat = np.zeros((NEXP, 512), np.float32)
    for r in range(512):
        emat[r // RNK, r] = 1.0
    wgT = C(np.asarray(L1_ffn_w_gate).T)
    wuT = C(np.asarray(L1_ffn_w_up).T)
    wdT = C(np.asarray(L1_ffn_w_down).T)
    ncF = _get("F", build_F)
    in_maps = []
    for core in range(8):
        b, h = core // 2, core % 2
        in_maps.append({
            "xT": C(np.asarray(L0_attn_x)[b].T),
            "wq": np.asarray(L0_attn_Wq), "wk": np.asarray(L0_attn_Wk),
            "wv": np.asarray(L0_attn_Wv), "wo": np.asarray(L0_attn_Wo),
            "wrT": wrT, "aT": aT, "bstT": bstT, "emat": emat,
            "onescol": onescol, "onesrow": onesrow,
            "hfin": np.full((128, 1), float(h), np.float32),
            "wq1": np.asarray(L1_attn_W_Q), "wk1": np.asarray(L1_attn_W_K),
            "wv1h": C(np.asarray(L1_attn_W_V)[:, 512 * h:512 * (h + 1)]),
            "wgT": wgT, "wuT": wuT, "wdT": wdT,
        })
    res = run_bass_kernel_spmd(ncF, in_maps, core_ids=list(range(8)))

    out = np.empty((B, T, D), np.float32)
    cl = np.arange(512)
    for core in range(8):
        b, h = core // 2, core % 2
        y = np.asarray(res.results[core]["x4T"])
        for w in range(4):
            i = 2048 * h + 4 * cl + w
            out[b, i, :] = y[:, 512 * w + cl].T
    return out
